# version 03 (final)
# baseline (speedup 1.0000x reference)
import numpy as np
import math
import contextlib
from collections import defaultdict
import concourse.bass as bass
import concourse.mybir as mybir

F32 = mybir.dt.float32
BF16 = mybir.dt.bfloat16
I32 = mybir.dt.int32
AF = mybir.ActivationFunctionType
ALU = mybir.AluOpType
AX = mybir.AxisListType

NCORE = 8
D = 1024
DK = D // 128

ENG_SEM = {'vector': 'vs', 'scalar': 'ss', 'tensor': 'ts', 'gpsimd': 'gs'}
DMA_SEM = {'sync': 'dsy', 'gpsimd': 'dgp', 'scalar': 'dsc', 'vector': 'dvc', 'tensor': 'dts'}


class Eng:

    def __init__(self, ctx, engname, raw):
        self.ctx = ctx
        self.engname = engname
        self.raw = raw
        self.esem = ENG_SEM.get(engname)
        self.dsem = DMA_SEM.get(engname)

    def wait(self, name, val=None):
        if val is None:
            val = self.ctx.count(name)
        if val <= 0:
            return
        key = (self.engname, name)
        if self.ctx._waited.get(key, -1) >= val:
            return
        self.raw.wait_ge(self.ctx.sem(name), val)
        self.ctx._waited[key] = val

    def op(self, inst, sem=None, k=1):
        sem = sem or self.esem
        inst.then_inc(self.ctx.sem(sem), k)
        self.ctx._ctr[sem] += k
        return self.ctx._ctr[sem]

    def dma(self, out, in_, sem=None, **kw):
        sem = sem or self.dsem
        inst = self.raw.dma_start(out=out, in_=in_, **kw)
        inst.then_inc(self.ctx.sem(sem), 16)
        self.ctx._ctr[sem] += 16
        return self.ctx._ctr[sem]

    def wait_lag(self, name, lag):
        v = self.ctx.count(name) - 16 * lag
        if v > 0:
            self.wait(name, v)

    def bump(self, name, k=1):
        inst = self.raw.sem_inc(self.ctx.sem(name), k)
        self.ctx._ctr[name] += k
        return self.ctx._ctr[name]

    def __getattr__(self, name):
        return getattr(self.raw, name)


class Ctx:
    def __init__(self, nc, es):
        self.nc = nc
        self.es = es
        self.block = es.enter_context(nc.Block())
        self._sems = {}
        self._ctr = defaultdict(int)
        self._waited = {}

    def sem(self, name):
        if name not in self._sems:
            self._sems[name] = self.es.enter_context(self.nc.semaphore(name))
        return self._sems[name]

    def count(self, name):
        return self._ctr[name]

    def sec(self, engname):
        def deco(fn):
            getattr(self.block, engname)(lambda raw: fn(Eng(self, engname, raw)))
        return deco

    def sbuf(self, name, shape, dtype=F32, es=None):
        return (es or self.es).enter_context(self.nc.sbuf_tensor(name, shape, dtype))

    def psum(self, name, shape, dtype=F32, es=None):
        return (es or self.es).enter_context(self.nc.psum_tensor(name, shape, dtype))



def emit_identity(ctx, name="ident", dtype=BF16, n=128):
    t = ctx.sbuf(name, [n, n], dtype)

    @ctx.sec('gpsimd')
    def _(g):
        g.op(g.memset(t[:], 0.0))
        g.wait('gs')
        g.op(g.affine_select(t[:], t[:], compare_op=ALU.not_equal, fill=1.0,
                             base=0, pattern=[[-1, n]], channel_multiplier=1))
    return t


def emit_ones_row(ctx, name="ones_row", n=1024, dtype=BF16, val=1.0):
    t = ctx.sbuf(name, [1, n], dtype)

    @ctx.sec('gpsimd')
    def _(g):
        g.op(g.memset(t[:], val))
    return t


def emit_load_x(ctx, x_tm, x_dram):
    @ctx.sec('sync')
    def _(s):
        s.dma(x_tm[:], x_dram.rearrange("(n p) d -> p n d", p=128))
    return ctx.count('dsy')


def emit_ln(ctx, x_tm, lnx16, NT, eps=1e-5, add_into_x=False, x_ready=(),
            prefix="", per_tile_hook=None, slot_guard=None, es=None, lnxf=None):
    cent = ctx.sbuf(prefix + "ln_cent", [128, 2, D], F32, es=es)
    sq = ctx.sbuf(prefix + "ln_sq", [128, D], F32, es=es)
    stat = ctx.sbuf(prefix + "ln_stat", [128, 8], F32, es=es)

    sq_marks = {}
    for t in range(NT):
        b = t % 2

        @ctx.sec('vector')
        def _(v, t=t, b=b):
            for nm, val in x_ready:
                v.wait(nm, val)
            if t >= 2:
                v.wait('ss', sq_marks[t - 2])
            v.op(v.reduce_sum(stat[:, 0:1], x_tm[:, t, :], axis=AX.X))
            v.wait('vs')
            v.op(v.tensor_scalar(stat[:, 1:2], stat[:, 0:1], 1.0 / D, None, ALU.mult))
            v.wait('vs')
            v.op(v.tensor_scalar(cent[:, b, :], x_tm[:, t, :], stat[:, 1:2], None, ALU.subtract))

        @ctx.sec('scalar')
        def _(s, t=t, b=b):
            s.wait('vs')
            s.wait('ss')
            s.op(s.activation(sq[:], cent[:, b, :], AF.Square, accum_out=stat[:, 2:3]))
            sq_marks[t] = ctx.count('ss')

        @ctx.sec('vector')
        def _(v, t=t, b=b):
            v.wait('ss', sq_marks[t])
            v.op(v.tensor_scalar(stat[:, 3:4], stat[:, 2:3], 1.0 / D, eps, ALU.mult, ALU.add))
            v.wait('vs')

        @ctx.sec('scalar')
        def _(s, t=t, b=b):
            s.wait('vs')
            s.op(s.activation(stat[:, 4:5], stat[:, 3:4], AF.Sqrt))

        @ctx.sec('vector')
        def _(v, t=t, b=b):
            v.wait('ss')
            v.op(v.reciprocal(stat[:, 5:6], stat[:, 4:5]))
            v.wait('vs')
            if slot_guard is not None:
                for nm, val in slot_guard(t):
                    v.wait(nm, val)
            v.op(v.tensor_scalar(lnx16[:, t % lnx16.shape[1], :], cent[:, b, :],
                                 stat[:, 5:6], None, ALU.mult))
            if lnxf is not None:
                v.wait('vs')
                v.op(v.tensor_scalar(lnxf[:, t % lnxf.shape[1], :], cent[:, b, :],
                                     stat[:, 5:6], None, ALU.mult))
            m = ctx.count('vs')
            if add_into_x:
                v.op(v.scalar_tensor_tensor(x_tm[:, t, :], cent[:, b, :], stat[:, 5:6],
                                            x_tm[:, t, :], ALU.mult, ALU.add))
        if per_tile_hook is not None:
            per_tile_hook(t, ctx.count('vs'))

    return ctx.count('vs')



def emit_moe(ctx, H, W, prefix="moe"):
    NT, T_LOC, T_GLOB = H['NT'], H['T_LOC'], H['T_GLOB']
    HID = 2048
    NH = HID // 128
    ident16, ones1024 = H['ident16'], H['ones1024']
    RSC = 4
    C_LOC = T_LOC // RSC

    w1sb = ctx.sbuf(prefix + "w1sb", [128, DK, HID], BF16)
    w2sb = ctx.sbuf(prefix + "w2sb", [128, NH, D], BF16)
    b1sb = ctx.sbuf(prefix + "b1sb", [1, HID], BF16)
    b2sb = ctx.sbuf(prefix + "b2sb", [1, D], BF16)
    onesf32 = ctx.sbuf(prefix + "onesf32", [1, 128], F32)

    @ctx.sec('gpsimd')
    def _(g):
        g.op(g.memset(onesf32[:], 1.0))
        g.dma(w1sb[:], W['W1_e'].rearrange("(c p) h -> p c h", p=128))
        g.dma(w2sb[:], W['W2_e'].rearrange("(c p) h -> p c h", p=128))
        g.dma(b1sb[:], W['b1_e'])
        g.dma(b2sb[:], W['b2_e'])
    wload = ctx.count('dgp')

    identf = H['identf']
    eselc = ctx.sbuf(prefix + "eselc", [8, 1], F32)
    gwf = ctx.sbuf(prefix + "gwf", [128, DK, 8], F32)

    @ctx.sec('gpsimd')
    def _(g):
        g.dma(eselc[:], W['eselc'], sem='dgw')
        g.dma(gwf[:], W['gate_W'].rearrange("(c p) e -> p c e", p=128), sem='dgw')
    gwload = ctx.count('dgw')

    phA = contextlib.ExitStack()
    x_tm = ctx.sbuf(prefix + "x_tm", [128, NT, D], F32, es=phA)
    lnx16 = ctx.sbuf(prefix + "lnx16", [128, 2, D], BF16, es=phA)
    lnxf = ctx.sbuf(prefix + "lnxf", [128, 2, D], F32, es=phA)
    lnxf_fm = ctx.sbuf(prefix + "lnxf_fm", [128, 2, DK, 128], F32, es=phA)
    tp_ev = ctx.sbuf(prefix + "tp_ev", [128, 4, 128], BF16, es=phA)
    logT = ctx.sbuf(prefix + "logT", [128, NT, 8], F32, es=phA)
    cwT_ev = ctx.sbuf(prefix + "cwT_ev", [8, 2, 128], F32, es=phA)
    ex = ctx.sbuf(prefix + "ex", [128, NT, 8], F32, es=phA)
    ex2 = ctx.sbuf(prefix + "ex2", [128, NT, 8], F32, es=phA)
    gst = ctx.sbuf(prefix + "gst", [128, NT, 8], F32, es=phA)
    cwl = ctx.sbuf(prefix + "cwl", [128, NT, 8], F32, es=phA)
    tp_ps = ctx.psum(prefix + "tp_ps", [128, 1024], BF16, es=phA)
    tpf_ps = ctx.psum(prefix + "tpf_ps", [128, 2, 512], F32, es=phA)
    lg_ps = ctx.psum(prefix + "lg_ps", [128, 2, 8], F32, es=phA)
    xm = emit_load_x(ctx, x_tm, W['x_in'])

    tp_marks = {}

    def tp_hook(t, mark):
        for c in range(DK):
            slot = (t * DK + c) % 4

            @ctx.sec('tensor')
            def _(te, t=t, c=c, slot=slot):
                te.wait('gs')
                te.wait('vs', mark)
                if (t * DK + c) >= 4:
                    te.wait('ss')
                te.op(te.transpose(tp_ps[:, slot * 128:(slot + 1) * 128],
                                   lnx16[:, t % 2, c * 128:(c + 1) * 128], ident16[:]))

            @ctx.sec('scalar')
            def _(s, t=t, c=c, slot=slot):
                s.wait('ts')
                s.wait('tpd%d' % slot)
                s.op(s.copy(tp_ev[:, slot, :], tp_ps[:, slot * 128:(slot + 1) * 128]))
                s.wait('ss')
                s.dma(W['lnag_in'][c * 128:(c + 1) * 128, t * 128:(t + 1) * 128],
                      tp_ev[:, slot, :], sem='tpd%d' % slot)
        for c in range(DK):
            slot = c % 2

            @ctx.sec('tensor')
            def _(te, t=t, c=c, slot=slot):
                te.wait('vs', mark)
                te.wait('vs')
                te.wait('ts')
                te.op(te.transpose(tpf_ps[:, slot, 0:128],
                                   lnxf[:, t % 2, c * 128:(c + 1) * 128], identf[:]))

            @ctx.sec('vector')
            def _(v, t=t, c=c, slot=slot):
                v.wait('ts')
                v.op(v.tensor_copy(lnxf_fm[:, t % 2, c, :], tpf_ps[:, slot, 0:128]))
        fm_mark2 = ctx.count('vs')

        @ctx.sec('tensor')
        def _(te, t=t, fm_mark2=fm_mark2):
            te.wait('vs', fm_mark2)
            te.wait('dgw', gwload)
            te.wait('vs')
            for c in range(DK):
                te.op(te.matmul(lg_ps[:, t % 2, :], lnxf_fm[:, t % 2, c, :],
                                gwf[:, c, :], start=(c == 0), stop=(c == DK - 1)))

        @ctx.sec('vector')
        def _(v, t=t):
            v.wait('ts')
            v.op(v.tensor_copy(logT[:, t, :], lg_ps[:, t % 2, :]))

        @ctx.sec('sync')
        def _(s, t=t, mark=mark):
            s.wait('vs')
            s.dma(W['x_out'][t * 128:(t + 1) * 128, :], x_tm[:, t, :])
            if 'dbg_log' in W:
                s.dma(W['dbg_log'][t * 128:(t + 1) * 128, :], logT[:, t, :])

    def slot_guard(t):
        g = []
        if t >= 2 and (t - 2) in tp_marks:
            g.append(('ts', tp_marks[t - 2]))
        return g

    def tp_hook2(t, mark):
        tp_hook(t, mark)
        tp_marks[t] = ctx.count('ts')

    emit_ln(ctx, x_tm, lnx16, NT, add_into_x=True, x_ready=[('dsy', xm)],
            prefix=prefix, per_tile_hook=tp_hook2, slot_guard=slot_guard, es=phA,
            lnxf=lnxf)
    xst_mark = ctx.count('dsy')
    logT_mark = ctx.count('vs')

    @ctx.sec('gpsimd')
    def _(g):
        for sl in range(4):
            g.wait('tpd%d' % sl)
        g.op(g.collective_compute(
            "AllGather", ALU.bypass, replica_groups=[list(range(NCORE))],
            ins=[W['lnag_in']], outs=[W['lnag_out']]), sem='cc')
    ag_mark = ctx.count('cc')

    @ctx.sec('vector')
    def _(v):
        v.wait('vs', logT_mark)
        lT = logT[:]
        e = ex[:]
        e2 = ex2[:]
        st = gst[:]
        nt = NT
        v.op(v.reduce_max(st[:, :, 0:1], lT, axis=AX.X))
        v.wait('vs')
        v.op(v.tensor_tensor(e, lT, st[:, :, 0:1].broadcast_to([128, nt, 8]),
                             ALU.subtract))
        v.wait('vs')

    @ctx.sec('scalar')
    def _(s):
        s.wait('vs')
        s.op(s.activation(ex[:], ex[:], AF.Exp))

    @ctx.sec('vector')
    def _(v):
        v.wait('ss')
        nt = NT
        e = ex[:]
        e2 = ex2[:]
        st = gst[:]
        v.op(v.reduce_sum(st[:, :, 1:2], e, axis=AX.X))
        v.wait('vs')
        v.op(v.tensor_scalar(e2, e, 1.0, None, ALU.is_lt))
        v.wait('vs')
        v.op(v.reduce_sum(st[:, :, 2:3], e2, axis=AX.X))
        v.wait('vs')
        v.op(v.tensor_tensor(e2, e, e2, ALU.mult))
        v.wait('vs')
        v.op(v.reduce_max(st[:, :, 3:4], e2, axis=AX.X))
        v.wait('vs')
        v.op(v.tensor_scalar(st[:, :, 4:5], st[:, :, 2:3], 6.5, None, ALU.is_le))
        v.wait('vs')
        v.op(v.tensor_tensor(st[:, :, 5:6], st[:, :, 3:4], st[:, :, 4:5], ALU.max))
        v.wait('vs')
        v.op(v.scalar_tensor_tensor(st[:, :, 6:7], st[:, :, 1:2], 1e-8,
                                    st[:, :, 5:6], ALU.mult, ALU.add))
        v.wait('vs')
        v.op(v.tensor_scalar(st[:, :, 6:7], st[:, :, 6:7], 1.0, None, ALU.add))
        v.wait('vs')
        v.op(v.reciprocal(st[:, :, 6:7], st[:, :, 6:7]))
        v.wait('vs')
        v.op(v.tensor_tensor(cwl[:], e, st[:, :, 5:6].broadcast_to([128, nt, 8]),
                             ALU.is_ge))
        v.wait('vs')
        v.op(v.tensor_tensor(cwl[:], cwl[:], e, ALU.mult))
        v.wait('vs')
        v.op(v.tensor_tensor(cwl[:], cwl[:], st[:, :, 6:7].broadcast_to([128, nt, 8]),
                             ALU.mult))
    cwl_mark = ctx.count('vs')

    for t in range(NT):
        sl = t % 2

        @ctx.sec('tensor')
        def _(te, t=t, sl=sl, cwl_mark=cwl_mark):
            te.wait('vs', cwl_mark)
            te.wait('ss')
            te.op(te.transpose(tpf_ps[0:8, sl, 0:128], cwl[:, t, :], identf[:]))

        @ctx.sec('scalar')
        def _(s, t=t, sl=sl):
            s.wait('ts')
            s.wait('cwx%d' % sl)
            s.op(s.copy(cwT_ev[:, sl, :], tpf_ps[0:8, sl, 0:128]))
            s.wait('ss')
            s.dma(W['cw_ag_in'][:, t * 128:(t + 1) * 128], cwT_ev[:, sl, :],
                  sem='cwx%d' % sl)

    @ctx.sec('gpsimd')
    def _(g):
        g.wait('cwx0')
        g.wait('cwx1')
        g.op(g.collective_compute(
            "AllGather", ALU.bypass, replica_groups=[list(range(NCORE))],
            ins=[W['cw_ag_in']], outs=[W['cw_ag_out']]), sem='cc')
        if 'dbg_cw' in W:
            g.wait('cc')
            g.dma(W['dbg_cw'], W['cw_ag_out'], sem='dbgd')
    cw_mark = ctx.count('cc')
    phA.close()


    phC = contextlib.ExitStack()
    cwblk = ctx.sbuf(prefix + "cwblk", [8, 2, 512], F32, es=phC)
    cwrow = ctx.sbuf(prefix + "cwrow", [1, 1024], BF16, es=phC)
    cwrow_f = ctx.sbuf(prefix + "cwrow_f", [1, 1024], F32, es=phC)
    cwb = ctx.sbuf(prefix + "cwb", [128, 1024], F32, es=phC)
    h16 = ctx.sbuf(prefix + "h16", [128, NH, 1024], BF16, es=phC)
    lnpair = ctx.sbuf(prefix + "lnpair", [128, DK, 1024], BF16, es=phC)
    po_sb = ctx.sbuf(prefix + "po_sb", [128, 4, 1024], F32, es=phC)
    cwrow_ps = ctx.psum(prefix + "cwrow_ps", [1, 1024], F32, es=phC)
    cwb_ps = ctx.psum(prefix + "cwb_ps", [128, 1024], F32, es=phC)
    h_ps = ctx.psum(prefix + "h_ps", [128, 2, 1024], F32, es=phC)
    o_ps = h_ps

    pairs = []
    for jc in range(RSC):
        for coff in range(0, C_LOC, 512):
            cw_len = min(512, C_LOC - coff)
            for s2 in range(NCORE // 2):
                pairs.append((jc, coff, cw_len, 2 * s2))

    rs_marks = []
    for pi, (jc, coff, cw_len, r) in enumerate(pairs):
        l0 = jc * C_LOC + coff
        W2TOK = 2 * cw_len

        @ctx.sec('sync')
        def _(s, pi=pi, l0=l0, cw_len=cw_len, r=r):
            s.wait('cc', ag_mark)
            s.wait('cc', cw_mark)
            s.wait('dsy', xst_mark)
            if pi >= 1:
                s.wait('ts')
            for rr in range(2):
                s.dma(lnpair[:, :, rr * cw_len:(rr + 1) * cw_len],
                      W['lnag_out'][(r + rr) * D:(r + rr + 1) * D, l0:l0 + cw_len]
                      .rearrange("(c p) t -> p c t", p=128), sem='lnp')
                s.dma(cwblk[0:8, rr, 0:cw_len],
                      W['cw_ag_out'][(r + rr) * 8:(r + rr) * 8 + 8, l0:l0 + cw_len],
                      sem='cwg')

        @ctx.sec('tensor')
        def _(te, pi=pi, cw_len=cw_len):
            te.wait('cwg')
            te.wait('dgw', gwload)
            if pi >= 1:
                te.wait('vs')
            for rr in range(2):
                te.op(te.matmul(cwrow_ps[:, rr * cw_len:(rr + 1) * cw_len],
                                eselc[:], cwblk[0:8, rr, 0:cw_len],
                                start=True, stop=True))

        @ctx.sec('vector')
        def _(v, pi=pi, cw_len=cw_len):
            v.wait('ts')
            v.op(v.tensor_copy(cwrow[:, 0:2 * cw_len], cwrow_ps[:, 0:2 * cw_len]))
            v.wait('vs')
            v.op(v.tensor_copy(cwrow_f[:, 0:2 * cw_len], cwrow_ps[:, 0:2 * cw_len]))

        @ctx.sec('tensor')
        def _(te, pi=pi, cw_len=cw_len):
            te.wait('vs')
            te.wait('gs')
            for off in range(0, 2 * cw_len, 512):
                w = min(512, 2 * cw_len - off)
                te.op(te.matmul(cwb_ps[:, off:off + w], onesf32[:],
                                cwrow_f[:, off:off + w], start=True, stop=True))

        @ctx.sec('vector')
        def _(v, pi=pi, cw_len=cw_len):
            v.wait('ts')
            v.op(v.tensor_copy(cwb[:, 0:2 * cw_len], cwb_ps[:, 0:2 * cw_len]))
        cwb_mark = ctx.count('vs')

        for m in range(NH):
            mb = m % 2

            @ctx.sec('tensor')
            def _(te, pi=pi, m=m, mb=mb, cw_len=cw_len):
                te.wait('lnp')
                te.wait('dgp', wload)
                if m < 2 and pi >= 1:
                    te.wait('ss')
                if m >= 2:
                    te.wait('vs')
                n = 2 * cw_len
                for off in range(0, n, 512):
                    w = min(512, n - off)
                    for c in range(DK):
                        te.op(te.matmul(h_ps[:, mb, off:off + w],
                                        w1sb[:, c, m * 128:(m + 1) * 128],
                                        lnpair[:, c, off:off + w],
                                        start=(c == 0), stop=False))
                    te.op(te.matmul(h_ps[:, mb, off:off + w],
                                    b1sb[:, m * 128:(m + 1) * 128],
                                    ones1024[:, off:off + w], start=False, stop=True))

            @ctx.sec('vector')
            def _(v, pi=pi, m=m, mb=mb, cw_len=cw_len):
                v.wait('ts')
                v.wait('vs', cwb_mark)
                n = 2 * cw_len
                v.op(v.scalar_tensor_tensor(h16[:, m, 0:n], h_ps[:, mb, 0:n], 0.0,
                                            cwb[:, 0:n], ALU.max, ALU.mult))
        h_mark = ctx.count('vs')

        nsub_total = W2TOK // 128
        for half in range((nsub_total + 1) // 2):
            subs = range(half * 2, min(nsub_total, half * 2 + 2))

            @ctx.sec('tensor')
            def _(te, pi=pi, subs=subs, half=half):
                te.wait('vs', h_mark)
                te.wait('ss')
                for si, sub in enumerate(subs):
                    for doff in range(0, D, 512):
                        for kc in range(NH):
                            te.op(te.matmul(o_ps[:, si, doff:doff + 512],
                                            h16[:, kc, sub * 128:(sub + 1) * 128],
                                            w2sb[:, kc, doff:doff + 512],
                                            start=(kc == 0), stop=False))
                        te.op(te.matmul(o_ps[:, si, doff:doff + 512],
                                        cwrow[:, sub * 128:(sub + 1) * 128],
                                        b2sb[:, doff:doff + 512], start=False, stop=True))

            @ctx.sec('scalar')
            def _(s, pi=pi, subs=subs, half=half, jc=jc, coff=coff, cw_len=cw_len, r=r):
                s.wait('ts')
                s.wait('pts%d' % (half % 2))
                for si, sub in enumerate(subs):
                    s.op(s.copy(po_sb[:, (half % 2) * 2 + si, :], o_ps[:, si, :]))
                s.wait('ss')
                for si, sub in enumerate(subs):
                    tok0 = sub * 128
                    rr = 1 if tok0 >= cw_len else 0
                    lrow = coff + (tok0 - rr * cw_len)
                    row0 = jc * NCORE * C_LOC + (r + rr) * C_LOC + lrow
                    s.dma(W['partials'][row0:row0 + 128, :],
                          po_sb[:, (half % 2) * 2 + si, :],
                          sem='pts%d' % (half % 2))

        if pi % (len(pairs) // RSC) == (len(pairs) // RSC) - 1:
            @ctx.sec('scalar')
            def _(s):
                s.wait('pts0')
                s.wait('pts1')
                s.op(s.copy(po_sb[0:1, 0, 0:1], po_sb[0:1, 0, 0:1]))

            @ctx.sec('gpsimd')
            def _(g, jc=jc):
                g.wait('ss')
                rows = NCORE * C_LOC
                g.op(g.collective_compute(
                    "ReduceScatter", ALU.add, replica_groups=[list(range(NCORE))],
                    ins=[W['partials'][jc * rows:(jc + 1) * rows, :]],
                    outs=[W['rs_out'][jc * C_LOC:(jc + 1) * C_LOC, :]]), sem='cc')
            rs_marks.append(ctx.count('cc'))
    phC.close()

    rsb = ctx.sbuf(prefix + "rsb", [128, 2, D], F32)
    xb2 = ctx.sbuf(prefix + "xb2", [128, 2, D], F32)
    for t in range(NT):
        bsel = t % 2

        @ctx.sec('sync')
        def _(s, t=t, bsel=bsel):
            s.wait('ts')
            s.wait('vs')
            s.wait('ss')
            s.wait('cc', rs_marks[min(t * 128 // C_LOC, RSC - 1)])
            s.wait('dsy', xst_mark)
            if t >= 2:
                s.wait('vs')
                s.wait('xfs%d' % bsel)
            s.dma(rsb[:, bsel, :], W['rs_out'][t * 128:(t + 1) * 128, :], sem='fin%d' % bsel)
            s.dma(xb2[:, bsel, :], W['x_out'][t * 128:(t + 1) * 128, :], sem='fin%d' % bsel)

        @ctx.sec('vector')
        def _(v, t=t, bsel=bsel):
            v.wait('fin%d' % bsel)
            if t >= 2:
                v.wait('xfs%d' % bsel)
            v.op(v.tensor_add(xb2[:, bsel, :], xb2[:, bsel, :], rsb[:, bsel, :]))

        @ctx.sec('scalar')
        def _(s, t=t, bsel=bsel):
            s.wait('vs')
            s.dma(W['x_out'][t * 128:(t + 1) * 128, :], xb2[:, bsel, :], sem='xfs%d' % bsel)
    if 'dbg_part' in W:
        @ctx.sec('gpsimd')
        def _(g):
            g.wait('cc')
            g.dma(W['dbg_part'], W['partials'], sem='dbgd')
            g.wait('dbgd')
    return (ctx.count('xfs0'), ctx.count('xfs1'))


GAMMA = 0.96875
RET_WIN = 4096


def emit_ln_to_fm(ctx, W, H, lnx_fm, prefix, add_into_x=False, store_x=False,
                  es=None):
    NT = H['NT']
    ident16 = H['ident16']
    ph = contextlib.ExitStack()
    x_tm = ctx.sbuf(prefix + "x_tm", [128, NT, D], F32, es=ph)
    lnx16 = ctx.sbuf(prefix + "lnx16", [128, 2, D], BF16, es=ph)
    tp_ps = ctx.psum(prefix + "tp_ps", [128, 1024], BF16, es=ph)
    xm = emit_load_x(ctx, x_tm, W['x_in'])

    tp_marks = {}

    def tp_hook(t, mark):
        for c in range(DK):
            slot = (t * DK + c) % 4

            @ctx.sec('tensor')
            def _(te, t=t, c=c, slot=slot):
                te.wait('gs')
                te.wait('vs', mark)
                if (t * DK + c) >= 4:
                    te.wait('ss')
                te.op(te.transpose(tp_ps[:, slot * 128:(slot + 1) * 128],
                                   lnx16[:, t % 2, c * 128:(c + 1) * 128], ident16[:]))

            @ctx.sec('scalar')
            def _(s, t=t, c=c, slot=slot):
                s.wait('ts')
                s.op(s.copy(lnx_fm[:, c, t * 128:(t + 1) * 128],
                            tp_ps[:, slot * 128:(slot + 1) * 128]))
        tp_marks[t] = ctx.count('ts')
        if store_x:
            @ctx.sec('sync')
            def _(s, t=t, mark=mark):
                s.wait('vs')
                s.dma(W['x_out'][t * 128:(t + 1) * 128, :], x_tm[:, t, :], sem='xst')

    def slot_guard(t):
        if t >= 2 and (t - 2) in tp_marks:
            return [('ts', tp_marks[t - 2])]
        return []

    emit_ln(ctx, x_tm, lnx16, NT, add_into_x=add_into_x, x_ready=[('dsy', xm)],
            prefix=prefix, per_tile_hook=tp_hook, slot_guard=slot_guard, es=ph)
    fm_mark = ctx.count('ss')
    ph.close()
    return fm_mark


def emit_retnet_proj(ctx, H, W, prefix="rp"):
    NT, T_LOC = H['NT'], H['T_LOC']
    wq = ctx.sbuf(prefix + "wq", [128, DK, D], BF16)
    wk = ctx.sbuf(prefix + "wk", [128, DK, D], BF16)
    wv = ctx.sbuf(prefix + "wv", [128, DK, D], BF16)
    thc = ctx.sbuf(prefix + "thc", [128, DK], F32)
    lnx_fm = ctx.sbuf(prefix + "lnx_fm", [128, DK, T_LOC], BF16)

    @ctx.sec('gpsimd')
    def _(g):
        g.dma(wq[:], W['Wq'].rearrange("(c p) m -> p c m", p=128))
        g.dma(wk[:], W['Wk'].rearrange("(c p) m -> p c m", p=128))
        g.dma(wv[:], W['Wv'].rearrange("(c p) m -> p c m", p=128))
        g.dma(thc[:].unsqueeze(2), W['theta'].rearrange("(c p) one -> p c one", p=128),
              allow_slow_non_contiguous=True)
    wload = ctx.count('dgp')

    @ctx.sec('scalar')
    def _(s):
        s.wait('dgp', wload)
        s.op(s.activation(thc[:], thc[:], AF.Exp, scale=0.0))
    th_mark = ctx.count('ss')

    fm_mark = emit_ln_to_fm(ctx, W, H, lnx_fm, prefix)

    ps = ctx.psum(prefix + "ps", [128, 2, 512], F32)
    ev = ctx.sbuf(prefix + "ev", [128, 2, 512], BF16)
    NQ = T_LOC // 512
    for name, wsb, dst in (("q", wq, W['q_out']), ("k", wk, W['k_out'])):
        for m in range(DK):
            for j in range(NQ):
                sl = (m * NQ + j) % 2

                @ctx.sec('tensor')
                def _(te, m=m, j=j, sl=sl, wsb=wsb):
                    te.wait('ss', fm_mark)
                    te.wait('dgp', wload)
                    te.wait('ss')
                    for c in range(DK):
                        te.op(te.matmul(ps[:, sl, :], wsb[:, c, m * 128:(m + 1) * 128],
                                        lnx_fm[:, c, j * 512:(j + 1) * 512],
                                        start=(c == 0), stop=(c == DK - 1)))

                @ctx.sec('scalar')
                def _(s, m=m, j=j, sl=sl, name=name):
                    s.wait('ts')
                    s.wait('ss', th_mark)
                    s.wait('pe%d' % sl)
                    s.op(s.activation(ev[:, sl, :], ps[:, sl, :], AF.Copy,
                                      scale=thc[:, m:m + 1]))
                    s.wait('ss')
                    s.dma(dst[m * 128:(m + 1) * 128, j * 512:(j + 1) * 512],
                          ev[:, sl, :], sem='pe%d' % sl)
    for t in range(NT):
        for half in range(2):
            sl = (t * 2 + half) % 2

            @ctx.sec('tensor')
            def _(te, t=t, half=half, sl=sl):
                te.wait('ss', fm_mark)
                te.wait('dgp', wload)
                te.wait('ss')
                for c in range(DK):
                    te.op(te.matmul(ps[:, sl, :], lnx_fm[:, c, t * 128:(t + 1) * 128],
                                    wv[:, c, half * 512:(half + 1) * 512],
                                    start=(c == 0), stop=(c == DK - 1)))

            @ctx.sec('scalar')
            def _(s, t=t, half=half, sl=sl):
                s.wait('ts')
                s.wait('pe%d' % sl)
                s.op(s.copy(ev[:, sl, :], ps[:, sl, :]))
                s.wait('ss')
                s.dma(W['v_out'][t * 128:(t + 1) * 128, half * 512:(half + 1) * 512],
                      ev[:, sl, :], sem='pe%d' % sl)
    return (ctx.count('pe0'), ctx.count('pe1'))


def emit_retnet_attn(ctx, H, W, prefix="ra"):
    NT, T_LOC = H['NT'], H['T_LOC']
    ident16, identf = H['ident16'], H['identf']
    CH = T_LOC // 2
    LNG = math.log(GAMMA)
    NSLOT = (RET_WIN + 512) // 128
    slot_lo = 0
    while GAMMA ** (RET_WIN - 128 * slot_lo - 511) < 1e-38:
        slot_lo += 1
    slots = list(range(slot_lo, NSLOT))
    NS = len(slots)

    b0 = ctx.sbuf(prefix + "b0", [128, 512], F32)
    b0m = ctx.sbuf(prefix + "b0m", [128, 4, 512], F32)

    @ctx.sec('gpsimd')
    def _(g):
        g.op(g.iota(b0[:], pattern=[[1, 512]], base=0, channel_multiplier=-1,
                    allow_small_or_imprecise_dtypes=True))
        for i in range(4):
            g.wait('gs')
            g.op(g.iota(b0m[:, i, :], pattern=[[1, 512]], base=0, channel_multiplier=-1,
                        allow_small_or_imprecise_dtypes=True))

    @ctx.sec('scalar')
    def _(s):
        s.wait('gs')
        s.op(s.activation(b0[:], b0[:], AF.Exp, scale=LNG))
        for i in range(4):
            s.wait('ss')
            s.op(s.activation(b0m[:, i, :], b0m[:, i, :], AF.Exp, scale=LNG))

    @ctx.sec('gpsimd')
    def _(g):
        g.wait('ss')
        for i in range(4):
            off = RET_WIN - 128 * (32 + i)
            g.op(g.affine_select(b0m[:, i, :], b0m[:, i, :], compare_op=ALU.is_ge,
                                 fill=0.0, base=off, pattern=[[1, 512]],
                                 channel_multiplier=-1))
    dmask_mark = ctx.count('gs')

    q_sb = ctx.sbuf(prefix + "q_sb", [128, DK, T_LOC], BF16)

    @ctx.sec('gpsimd')
    def _(g):
        g.dma(q_sb[:], W['q_fm'].rearrange("(c p) t -> p c t", p=128))
    qload = ctx.count('dgp')

    kbuf = ctx.sbuf(prefix + "kbuf", [128, 2, DK, 128], BF16)
    vbuf = ctx.sbuf(prefix + "vbuf", [128, 2, 512], BF16)
    sd16 = ctx.sbuf(prefix + "sd16", [128, NS, 512], BF16)
    aev = ctx.sbuf(prefix + "aev", [128, DK, 512], F32)
    xb = ctx.sbuf(prefix + "xb", [128, 2, D], F32)
    ps_sc = ctx.psum(prefix + "ps_sc", [128, 2, 512], F32)
    ps_pv = ctx.psum(prefix + "ps_pv", [128, 4, 512], F32)
    ps_tr = ctx.psum(prefix + "ps_tr", [128, 1024], F32)

    QPB = CH // 512
    for b in range(2):
        for qt in range(QPB):
            woff = qt * 512
            qglob = b * CH + qt * 512
            for si, sc in enumerate(slots):
                sl = si % 2

                @ctx.sec('sync')
                def _(s, b=b, woff=woff, sc=sc, sl=sl):
                    s.wait('dgp', qload)
                    s.wait('ts')
                    soff = woff + sc * 128
                    s.dma(kbuf[:, sl, :, :],
                          W['k_win'][b, :, soff:soff + 128]
                          .rearrange("(c p) t -> p c t", p=128), sem='kl%d' % sl)

                @ctx.sec('tensor')
                def _(te, qglob=qglob, sl=sl):
                    te.wait('kl%d' % sl)
                    te.wait('vs')
                    for c in range(DK):
                        te.op(te.matmul(ps_sc[:, sl, :], kbuf[:, sl, c, :],
                                        q_sb[:, c, qglob:qglob + 512],
                                        start=(c == 0), stop=(c == DK - 1)))

                @ctx.sec('vector')
                def _(v, sc=sc, sl=sl, si=si):
                    v.wait('gs', dmask_mark)
                    v.wait('ts')
                    c_sc = GAMMA ** (RET_WIN - 128 * sc)
                    dpat = b0m[:, sc - 32, :] if sc >= 32 else b0[:]
                    v.op(v.scalar_tensor_tensor(sd16[:, si, :], dpat, c_sc,
                                                ps_sc[:, sl, :], ALU.mult, ALU.mult))
            sd_mark = ctx.count('vs')

            for rnd in range(2):
                for si, sc in enumerate(slots):
                    sl = si % 2

                    @ctx.sec('sync')
                    def _(s, b=b, woff=woff, sc=sc, sl=sl, rnd=rnd):
                        s.wait('ts')
                        soff = woff + sc * 128
                        s.dma(vbuf[:, sl, :],
                              W['v_win'][b, soff:soff + 128,
                                         rnd * 512:(rnd + 1) * 512],
                              sem='vl%d' % sl)

                    @ctx.sec('tensor')
                    def _(te, si=si, sl=sl, rnd=rnd, sd_mark=sd_mark):
                        te.wait('vs', sd_mark)
                        te.wait('vl%d' % sl)
                        if si == 0:
                            te.wait('ss')
                        for vd in range(4):
                            te.op(te.matmul(ps_pv[:, vd, :],
                                            vbuf[:, sl, vd * 128:(vd + 1) * 128],
                                            sd16[:, si, :],
                                            start=(si == 0), stop=(si == NS - 1)))

                @ctx.sec('scalar')
                def _(s, rnd=rnd):
                    s.wait('ts')
                    for vd in range(4):
                        s.op(s.copy(aev[:, rnd * 4 + vd, :], ps_pv[:, vd, :]))
            aev_mark = ctx.count('ss')

            for sub in range(4):
                @ctx.sec('sync')
                def _(s, qglob=qglob, sub=sub):
                    s.wait('vs')
                    s.wait('xso%d' % (sub % 2))
                    r0 = qglob + sub * 128
                    s.dma(xb[:, sub % 2, :], W['x_in'][r0:r0 + 128, :],
                          sem='xl%d' % (sub % 2))

                @ctx.sec('tensor')
                def _(te, sub=sub, aev_mark=aev_mark):
                    te.wait('ss', aev_mark)
                    if sub >= 1:
                        te.wait('vs')
                    for vd in range(DK):
                        te.op(te.transpose(ps_tr[:, vd * 128:(vd + 1) * 128],
                                           aev[:, vd, sub * 128:(sub + 1) * 128],
                                           identf[:]))

                @ctx.sec('vector')
                def _(v, qglob=qglob, sub=sub):
                    v.wait('ts')
                    v.wait('xl%d' % (sub % 2))
                    v.op(v.tensor_add(xb[:, sub % 2, :], xb[:, sub % 2, :], ps_tr[:]))

                @ctx.sec('scalar')
                def _(s, qglob=qglob, sub=sub):
                    s.wait('vs')
                    r0 = qglob + sub * 128
                    s.dma(W['x_out'][r0:r0 + 128, :], xb[:, sub % 2, :],
                          sem='xso%d' % (sub % 2))
    return (ctx.count('xso0'), ctx.count('xso1'))


ATTN_PAD = -60000.0


def emit_attnb_proj(ctx, H, W, prefix="bp"):
    NT, T_LOC = H['NT'], H['T_LOC']
    ident16 = H['ident16']

    wq = ctx.sbuf(prefix + "wq", [128, DK, D], BF16)
    wkv = ctx.sbuf(prefix + "wkv", [128, DK, 128], BF16)
    wk = ctx.sbuf(prefix + "wk", [128, 512], BF16)
    wv = ctx.sbuf(prefix + "wv", [128, 512], BF16)
    qg8 = ctx.sbuf(prefix + "qg8", [128, D], F32)
    kgb = ctx.sbuf(prefix + "kgb", [128, 512], F32)
    lnx_fm = ctx.sbuf(prefix + "lnx_fm", [128, DK, T_LOC], BF16)
    kv_fm = ctx.sbuf(prefix + "kv_fm", [128, T_LOC], BF16)

    @ctx.sec('gpsimd')
    def _(g):
        g.dma(wq[:], W['Wq'].rearrange("(c p) m -> p c m", p=128))
        g.dma(wkv[:], W['Wkv'].rearrange("(c p) m -> p c m", p=128))
        g.dma(wk[:], W['Wk'])
        g.dma(wv[:], W['Wv'])
        g.dma(qg8[:], W['qg_row'].broadcast_to([128, D]))
        g.dma(kgb[:], W['kg_row'].broadcast_to([128, 512]))
    wload = ctx.count('dgp')

    @ctx.sec('vector')
    def _(v):
        v.wait('dgp', wload)
        v.op(v.tensor_scalar(qg8[:], qg8[:], 0.125, None, ALU.mult))
    qg8_mark = ctx.count('vs')

    fm_mark = emit_ln_to_fm(ctx, W, H, lnx_fm, prefix)

    ps = ctx.psum(prefix + "ps", [128, 2, 512], F32)
    ps_tp = ctx.psum(prefix + "ps_tp", [128, 2, 128], BF16)
    qtm = ctx.sbuf(prefix + "qtm", [128, 1024], F32)
    ktm = ctx.sbuf(prefix + "ktm", [128, 512], F32)
    sq = ctx.sbuf(prefix + "sq", [128, 1024], F32)
    asm = ctx.sbuf(prefix + "asm", [128, 16, 128], BF16)
    asmk = ctx.sbuf(prefix + "asmk", [128, 16, 128], BF16)
    stt_ = ctx.sbuf(prefix + "stt", [128, 16, 4], F32)
    aev = ctx.sbuf(prefix + "aev", [34, 2, 128], BF16)
    vev = ctx.sbuf(prefix + "vev", [128, 2, 512], BF16)

    @ctx.sec('gpsimd')
    def _(g):
        g.wait('vs')
        g.wait('ss')
        g.wait('ts')
        g.wait('dsy')
        g.op(g.memset(asm[:], 0.0))
        g.op(g.memset(asmk[:], 0.0))
        g.wait('gs')
        g.op(g.memset(asm[:, :, 33:34], 1.0))
        g.op(g.memset(asmk[:, :, 32:33], -1.0))
    asm_mark = ctx.count('gs')

    NQ = T_LOC // 512
    for j in range(NQ):
        @ctx.sec('tensor')
        def _(te, j=j):
            te.wait('ss', fm_mark)
            te.wait('dgp', wload)
            te.wait('vs')
            for c in range(DK):
                te.op(te.matmul(ps[:, 0, :], wkv[:, c, :],
                                lnx_fm[:, c, j * 512:(j + 1) * 512],
                                start=(c == 0), stop=(c == DK - 1)))

        @ctx.sec('vector')
        def _(v, j=j):
            v.wait('ts')
            v.op(v.tensor_copy(kv_fm[:, j * 512:(j + 1) * 512], ps[:, 0, :]))
    kv_mark = ctx.count('vs')

    for t in range(NT):
        @ctx.sec('tensor')
        def _(te, t=t):
            te.wait('ss', fm_mark)
            te.wait('dgp', wload)
            te.wait('vs')
            te.wait('ss')
            for half in range(2):
                for c in range(DK):
                    te.op(te.matmul(ps[:, half, :], lnx_fm[:, c, t * 128:(t + 1) * 128],
                                    wq[:, c, half * 512:(half + 1) * 512],
                                    start=(c == 0), stop=(c == DK - 1)))

        @ctx.sec('vector')
        def _(v, t=t):
            v.wait('ts')
            v.wait('vs', qg8_mark)
            v.op(v.tensor_copy(qtm[:, 0:512], ps[:, 0, :]))
            v.op(v.tensor_copy(qtm[:, 512:1024], ps[:, 1, :]))
            v.wait('vs')

        @ctx.sec('scalar')
        def _(s, t=t):
            s.wait('vs')
            s.op(s.activation(sq[:], qtm[:], AF.Square))

        @ctx.sec('vector')
        def _(v, t=t):
            v.wait('ss')
            s3 = sq[:].rearrange("p (h d) -> p h d", h=16)
            v.op(v.raw.tensor_reduce(stt_[:, :, 0:1], s3, AX.X, ALU.add))
            v.wait('vs')
            v.op(v.tensor_scalar(stt_[:, :, 1:2], stt_[:, :, 0:1], 1.0 / 64, 1e-6,
                                 ALU.mult, ALU.add))
            v.wait('vs')

        @ctx.sec('scalar')
        def _(s, t=t):
            s.wait('vs')
            s.op(s.activation(stt_[:, :, 2:3], stt_[:, :, 1:2], AF.Sqrt))

        @ctx.sec('vector')
        def _(v, t=t):
            v.wait('ss')
            v.op(v.reciprocal(stt_[:, :, 3:4], stt_[:, :, 2:3]))
            v.wait('vs')
            v.wait('gs', asm_mark)
            v.wait('ts')
            q3 = qtm[:].rearrange("p (h d) -> p h d", h=16)
            g3 = qg8[:].rearrange("p (h d) -> p h d", h=16)
            v.op(v.tensor_tensor(sq[:].rearrange("p (h d) -> p h d", h=16)[:, :, 0:32],
                                 q3[:, :, 0:32],
                                 stt_[:, :, 3:4].broadcast_to([128, 16, 32]), ALU.mult))
            v.wait('vs')
            v.op(v.tensor_tensor(asm[:, :, 0:32],
                                 sq[:].rearrange("p (h d) -> p h d", h=16)[:, :, 0:32],
                                 g3[:, :, 0:32], ALU.mult))
        qasm_mark = ctx.count('vs')

        for h in range(16):
            sl = h % 2

            @ctx.sec('tensor')
            def _(te, t=t, h=h, sl=sl, qasm_mark=qasm_mark):
                te.wait('vs', qasm_mark)
                te.wait('ss')
                te.op(te.transpose(ps_tp[:, sl, :], asm[:, h, :], ident16[:]))

            @ctx.sec('scalar')
            def _(s, t=t, h=h, sl=sl):
                s.wait('ts')
                s.wait('qx%d' % sl)
                s.op(s.copy(aev[:, sl, :], ps_tp[0:34, sl, :]))
                s.wait('ss')
                s.dma(W['q_pre'][h, :, t * 128:(t + 1) * 128], aev[:, sl, :],
                      sem='qx%d' % sl)

        @ctx.sec('tensor')
        def _(te, t=t):
            te.wait('vs', kv_mark)
            te.wait('vs')
            te.op(te.matmul(ps[:, 0, :], kv_fm[:, t * 128:(t + 1) * 128], wk[:],
                            start=True, stop=True))

        @ctx.sec('vector')
        def _(v, t=t):
            v.wait('ts')
            v.op(v.tensor_copy(ktm[:], ps[:, 0, :]))
            v.wait('vs')

        @ctx.sec('scalar')
        def _(s, t=t):
            s.wait('vs')
            s.op(s.activation(sq[:, 0:512], ktm[:], AF.Square))

        @ctx.sec('vector')
        def _(v, t=t):
            v.wait('ss')
            s3 = sq[:, 0:512].rearrange("p (h d) -> p h d", h=16)
            v.op(v.raw.tensor_reduce(stt_[:, :, 0:1], s3, AX.X, ALU.add))
            v.wait('vs')
            v.op(v.tensor_scalar(stt_[:, :, 1:2], stt_[:, :, 0:1], 1.0 / 32, 1e-6,
                                 ALU.mult, ALU.add))
            v.wait('vs')

        @ctx.sec('scalar')
        def _(s, t=t):
            s.wait('vs')
            s.op(s.activation(stt_[:, :, 2:3], stt_[:, :, 1:2], AF.Sqrt))

        @ctx.sec('vector')
        def _(v, t=t):
            v.wait('ss')
            v.op(v.reciprocal(stt_[:, :, 3:4], stt_[:, :, 2:3]))
            v.wait('vs')
            v.wait('ts')
            k3 = ktm[:].rearrange("p (h d) -> p h d", h=16)
            g3 = kgb[:].rearrange("p (h d) -> p h d", h=16)
            v.op(v.tensor_tensor(sq[:, 0:512].rearrange("p (h d) -> p h d", h=16),
                                 k3, stt_[:, :, 3:4].broadcast_to([128, 16, 32]),
                                 ALU.mult))
            v.wait('vs')
            v.op(v.tensor_tensor(asmk[:, :, 0:32],
                                 sq[:, 0:512].rearrange("p (h d) -> p h d", h=16),
                                 g3, ALU.mult))
        kasm_mark = ctx.count('vs')

        for h in range(16):
            sl = h % 2

            @ctx.sec('tensor')
            def _(te, t=t, h=h, sl=sl, kasm_mark=kasm_mark):
                te.wait('vs', kasm_mark)
                te.wait('ss')
                te.op(te.transpose(ps_tp[:, sl, :], asmk[:, h, :], ident16[:]))

            @ctx.sec('scalar')
            def _(s, t=t, h=h, sl=sl):
                s.wait('ts')
                s.wait('qx%d' % sl)
                s.op(s.copy(aev[:, sl, :], ps_tp[0:34, sl, :]))
                s.wait('ss')
                s.dma(W['k_pre'][h, :, t * 128:(t + 1) * 128], aev[:, sl, :],
                      sem='qx%d' % sl)

        @ctx.sec('tensor')
        def _(te, t=t):
            te.wait('vs', kv_mark)
            te.wait('ss')
            te.op(te.matmul(ps[:, 1, :], kv_fm[:, t * 128:(t + 1) * 128], wv[:],
                            start=True, stop=True))

        @ctx.sec('scalar')
        def _(s, t=t):
            s.wait('ts')
            s.wait('vx%d' % (t % 2))
            s.op(s.copy(vev[:, t % 2, :], ps[:, 1, :]))
            s.wait('ss')
            s.dma(W['v_pre'][t * 128:(t + 1) * 128, :], vev[:, t % 2, :],
                  sem='vx%d' % (t % 2))
    return (ctx.count('qx0'), ctx.count('qx1'), ctx.count('vx0'), ctx.count('vx1'))


def emit_attnb_attn(ctx, H, W, prefix="ba", nheads=16, ndim=32, win_seq=None):
    NT, T_LOC = H['NT'], H['T_LOC']
    ident16 = H['ident16']
    CH = T_LOC // 2
    WSEQ = win_seq
    WIN = WSEQ + CH
    QPB = CH // 512
    NQT8 = CH // 128

    wo_sb = ctx.sbuf(prefix + "wo", [128, DK, D], BF16)

    @ctx.sec('gpsimd')
    def _(g):
        g.dma(wo_sb[:], W['wo_pk'].rearrange("(c p) m -> p c m", p=128))
    wload = ctx.count('dgp')

    msk = ctx.sbuf(prefix + "msk", [128, 4, 512], F32)

    @ctx.sec('gpsimd')
    def _(g):
        g.op(g.memset(msk[:], 1.0))
        for j in range(4):
            g.wait('gs')
            g.op(g.affine_select(msk[:, j, :], msk[:, j, :], compare_op=ALU.is_ge,
                                 fill=0.0, base=-128 * j, pattern=[[1, 512]],
                                 channel_multiplier=-1))
    msk_mark = ctx.count('gs')

    att16 = ctx.sbuf(prefix + "att16", [128, DK, T_LOC], BF16)

    @ctx.sec('gpsimd')
    def _(g):
        g.wait('gs', msk_mark)
        g.op(g.memset(att16[:], 0.0))
    att0_mark = ctx.count('gs')

    q_hb = ctx.sbuf(prefix + "q_hb", [34, 2, CH], BF16)
    kwb = ctx.sbuf(prefix + "kwb", [34, 2, 2048], BF16)
    vwb = ctx.sbuf(prefix + "vwb", [128, 2, 8, 33], BF16)
    rm = ctx.sbuf(prefix + "rm", [128, 8, 17], F32)
    rmx16 = ctx.sbuf(prefix + "rmx16", [128, 8], BF16)
    rrow = ctx.sbuf(prefix + "rrow", [1, 2, 128], BF16)
    pb16 = ctx.sbuf(prefix + "pb16", [128, 4, 512], BF16)
    den = ctx.sbuf(prefix + "den", [1, 512], F32)
    rdb_sb = ctx.sbuf(prefix + "rdb", [32, 512], F32)
    ps_sc = ctx.psum(prefix + "ps_sc", [128, 4, 512], F32)
    ps_pv = ctx.psum(prefix + "ps_pv", [33, 512], F32)
    ph_ms = contextlib.ExitStack()
    ps_ms = ctx.psum(prefix + "ps_ms", [128, 512], F32, es=ph_ms)
    ps_rd = ctx.psum(prefix + "ps_rd", [32, 512], F32, es=ph_ms)

    NB_A = WIN // 512

    for b in range(2):
        for h in range(nheads):
            ih = (b * nheads + h)
            qs = ih % 2

            @ctx.sec('sync')
            def _(s, b=b, h=h, qs=qs):
                s.wait('ts')
                s.dma(q_hb[:, qs, :], W['q_pre'][h, :, b * CH:(b + 1) * CH],
                      sem='qh%d' % qs)

            for bg in range(0, NB_A, 4):
                nb = min(4, NB_A - bg)
                ks = (bg // 4) % 2

                @ctx.sec('sync')
                def _(s, b=b, h=h, bg=bg, nb=nb, ks=ks):
                    s.wait('ts')
                    s.dma(kwb[:, ks, 0:nb * 512],
                          W['k_win'][b, h, :, bg * 512:(bg + nb) * 512],
                          sem='kb%d' % ks)

                for qt in range(NQT8):
                    @ctx.sec('tensor')
                    def _(te, qt=qt, bg=bg, nb=nb, ks=ks, qs=qs):
                        te.wait('qh%d' % qs)
                        te.wait('kb%d' % ks)
                        te.wait('vs')
                        for k in range(nb):
                            te.op(te.matmul(ps_sc[:, k, :],
                                            q_hb[0:ndim, qs, qt * 128:(qt + 1) * 128],
                                            kwb[0:ndim, ks, k * 512:(k + 1) * 512],
                                            start=True, stop=True))

                    @ctx.sec('vector')
                    def _(v, qt=qt, bg=bg, nb=nb):
                        v.wait('ts')
                        v.op(v.reduce_max(rm[:, qt, bg // 4:bg // 4 + 1],
                                          ps_sc[:, 0:nb, :], axis=AX.XY))

            NGA = (NB_A + 3) // 4
            for qt in range(NQT8):
                @ctx.sec('vector')
                def _(v, qt=qt):
                    v.wait('vs')
                    v.op(v.reduce_max(rm[:, qt, 16:17], rm[:, qt, 0:NGA], axis=AX.X))
                    v.wait('vs')
                    v.op(v.tensor_copy(rmx16[:, qt:qt + 1], rm[:, qt, 16:17]))

            for qt in range(NQT8):
                sl = qt % 2

                @ctx.sec('tensor')
                def _(te, qt=qt, sl=sl):
                    te.wait('gs')
                    te.wait('vs')
                    te.wait('ss')
                    te.op(te.matmul(ps_ms[0:1, sl * 128:(sl + 1) * 128],
                                    rmx16[:, qt:qt + 1], ident16[:],
                                    start=True, stop=True))

                @ctx.sec('scalar')
                def _(s, qt=qt, sl=sl, qs=qs):
                    s.wait('ts')
                    s.op(s.copy(rrow[:, sl, :], ps_ms[0:1, sl * 128:(sl + 1) * 128]))
                    s.wait('ss')
                    s.op(s.copy(q_hb[32:33, qs, qt * 128:(qt + 1) * 128], rrow[:, sl, :]))
            qaug_mark = ctx.count('ss')

            for qt in range(QPB):
                nsc = (WSEQ + qt * 512 + 512) // 128
                for scq in range(0, nsc, 4):
                    ks = (scq // 8) % 2
                    if scq % 8 == 0:
                        ldw_ = min(1024, WIN - scq * 128)

                        @ctx.sec('sync')
                        def _(s, b=b, h=h, scq=scq, ldw_=ldw_, ks=ks):
                            s.wait('ts')
                            s.dma(kwb[:, ks, 0:ldw_],
                                  W['k_win'][b, h, :, scq * 128:scq * 128 + ldw_],
                                  sem='kb%d' % ks)
                            s.dma(vwb[:, ks, 0:ldw_ // 128, :],
                                  W['v_win'][b, scq * 128:scq * 128 + ldw_, h, :]
                                  .rearrange("(k p) c -> p k c", p=128),
                                  sem='kb%d' % ks)

                    last = (scq + 4 >= nsc)

                    @ctx.sec('tensor')
                    def _(te, qt=qt, scq=scq, ks=ks, qs=qs, qaug_mark=qaug_mark):
                        te.wait('ss', qaug_mark)
                        te.wait('kb%d' % ks)
                        te.wait('ss')
                        for k in range(4):
                            ksl = (scq + k) % 8
                            te.op(te.matmul(ps_sc[:, k, :],
                                            kwb[:, ks, ksl * 128:(ksl + 1) * 128],
                                            q_hb[:, qs, qt * 512:(qt + 1) * 512],
                                            start=True, stop=True))

                    @ctx.sec('scalar')
                    def _(s, scq=scq):
                        s.wait('ts')
                        s.wait('vs')
                        s.op(s.activation(pb16[:, :, :], ps_sc[:, :, :], AF.Exp))

                    if last:
                        @ctx.sec('vector')
                        def _(v):
                            v.wait('ss')
                            v.wait('gs', msk_mark)
                            v.op(v.tensor_tensor(pb16[:, :, :], pb16[:, :, :],
                                                 msk[:, :, :], ALU.mult))

                    @ctx.sec('tensor')
                    def _(te, scq=scq, nsc=nsc, last=last, ks=ks):
                        if last:
                            te.wait('vs')
                        else:
                            te.wait('ss')
                        if scq == 0:
                            te.wait('vs')
                        for k in range(4):
                            ksl = (scq + k) % 8
                            te.op(te.matmul(ps_pv[:, :], vwb[:, ks, ksl, :],
                                            pb16[:, k, :],
                                            start=(scq + k == 0),
                                            stop=(scq + k == nsc - 1)))

                @ctx.sec('scalar')
                def _(s, qt=qt):
                    s.wait('ts')
                    s.op(s.copy(den[:], ps_pv[32:33, :]))

                @ctx.sec('vector')
                def _(v, qt=qt):
                    v.wait('ss')
                    v.op(v.reciprocal(den[:], den[:]))
                den_mark = ctx.count('vs')

                @ctx.sec('tensor')
                def _(te, qt=qt, den_mark=den_mark):
                    te.wait('vs', den_mark)
                    te.wait('vs')
                    te.op(te.matmul(ps_rd[:, :], H['ones32'][:], den[:],
                                    start=True, stop=True))

                @ctx.sec('vector')
                def _(v, b=b, h=h, qt=qt):
                    v.wait('ts')
                    v.op(v.tensor_copy(rdb_sb[:], ps_rd[:, :]))
                    v.wait('vs')
                    v.wait('gs', att0_mark)
                    grp = h // 2
                    ro = (h % 2) * 64
                    col = b * CH + qt * 512
                    v.op(v.tensor_tensor(att16[ro:ro + 32, grp, col:col + 512],
                                         ps_pv[0:32, :], rdb_sb[:], ALU.mult))
    attn_mark = ctx.count('vs')
    ph_ms.close()

    xb = ctx.sbuf(prefix + "xb", [128, 2, D], F32)
    ps_o = ctx.psum(prefix + "ps_o", [128, 2, 512], F32)
    for t in range(NT):
        bsl = t % 2

        @ctx.sec('sync')
        def _(s, t=t, bsl=bsl):
            s.wait('vs')
            s.wait('xo%d' % bsl)
            s.dma(xb[:, bsl, :], W['x_in'][t * 128:(t + 1) * 128, :],
                  sem='xl%d' % bsl)

        @ctx.sec('tensor')
        def _(te, t=t, bsl=bsl):
            te.wait('vs', attn_mark)
            te.wait('dgp', wload)
            te.wait('vs')
            for half in range(2):
                for c in range(DK):
                    te.op(te.matmul(ps_o[:, bsl, :] if False else ps_o[:, half, :],
                                    att16[:, c, t * 128:(t + 1) * 128],
                                    wo_sb[:, c, half * 512:(half + 1) * 512],
                                    start=(c == 0), stop=(c == DK - 1)))

        @ctx.sec('vector')
        def _(v, t=t, bsl=bsl):
            v.wait('ts')
            v.wait('xl%d' % bsl)
            v.op(v.tensor_add(xb[:, bsl, 0:512], xb[:, bsl, 0:512], ps_o[:, 0, :]))
            v.wait('vs')
            v.op(v.tensor_add(xb[:, bsl, 512:1024], xb[:, bsl, 512:1024], ps_o[:, 1, :]))

        @ctx.sec('scalar')
        def _(s, t=t, bsl=bsl):
            s.wait('vs')
            s.dma(W['x_out'][t * 128:(t + 1) * 128, :], xb[:, bsl, :],
                  sem='xo%d' % bsl)
    return (ctx.count('xo0'), ctx.count('xo1'))



def emit_attna_proj(ctx, H, W, prefix="ap"):
    NT, T_LOC = H['NT'], H['T_LOC']
    ident16 = H['ident16']

    wqkv = ctx.sbuf(prefix + "wqkv", [64, 192], BF16)
    bqkv = ctx.sbuf(prefix + "bqkv", [1, 192], BF16)
    ones1 = ctx.sbuf(prefix + "ones1", [1, 128], BF16)
    onesc = ctx.sbuf(prefix + "onesc", [128, 1], BF16)
    lnx_fm = ctx.sbuf(prefix + "lnx_fm", [128, DK, T_LOC], BF16)
    lnx_fm2 = ctx.sbuf(prefix + "lnx_fm2", [128, DK, T_LOC], BF16)

    @ctx.sec('gpsimd')
    def _(g):
        g.dma(wqkv[:], W['Wqkv'])
        g.dma(bqkv[:], W['bqkv'])
        g.op(g.memset(ones1[:], 1.0))
        g.op(g.memset(onesc[:], 1.0))
    wload = ctx.count('dgp')

    fm_mark = emit_ln_to_fm(ctx, W, H, lnx_fm, prefix)

    @ctx.sec('gpsimd')
    def _(g):
        g.wait('ss', fm_mark)
        for c in range(DK):
            g.dma(lnx_fm2[0:64, c, :], lnx_fm[64:128, c, :])
            if c + 1 < DK:
                g.dma(lnx_fm2[64:128, c, :], lnx_fm[0:64, c + 1, :])
            else:
                g.op(g.memset(lnx_fm2[64:128, c, :], 0.0))
    fm2_mark = max(ctx.count('dgp'), 0)
    fm2g = ctx.count('gs')

    ps = ctx.psum(prefix + "ps", [128, 2, 192], F32)
    ps_tp = ctx.psum(prefix + "ps_tp", [128, 2, 128], BF16)
    ps_s = ctx.psum(prefix + "ps_s", [1, 1024], F32)
    asm = ctx.sbuf(prefix + "asm", [128, 16, 128], BF16)
    asmk = ctx.sbuf(prefix + "asmk", [128, 16, 128], BF16)
    vev = ctx.sbuf(prefix + "vev", [128, 16, 64], BF16)
    aev = ctx.sbuf(prefix + "aev", [66, 2, 128], BF16)
    tsev = ctx.sbuf(prefix + "tsev", [1, 2, 1024], F32)

    @ctx.sec('gpsimd')
    def _(g):
        g.wait('vs')
        g.wait('ss')
        g.wait('ts')
        g.wait('dsy')
        g.op(g.memset(asm[:], 0.0))
        g.op(g.memset(asmk[:], 0.0))
        g.wait('gs')
        g.op(g.memset(asm[:, :, 65:66], 1.0))
        g.op(g.memset(asmk[:, :, 64:65], -1.0))
    asm_mark = ctx.count('gs')

    for t in range(NT):
        for h in range(16):
            hs = h % 2

            @ctx.sec('tensor')
            def _(te, t=t, h=h, hs=hs):
                te.wait('ss', fm_mark)
                te.wait('dgp')
                te.wait('gs', fm2g)
                te.wait('ss')
                src = lnx_fm if h % 2 == 0 else lnx_fm2
                te.op(te.matmul(ps[:, hs, :], src[0:64, h // 2, t * 128:(t + 1) * 128],
                                wqkv[:], start=True, stop=False))
                te.op(te.matmul(ps[:, hs, :], ones1[:], bqkv[:],
                                start=False, stop=True))

            @ctx.sec('scalar')
            def _(s, t=t, h=h, hs=hs):
                s.wait('ts')
                s.wait('gs', asm_mark)
                s.wait('vx0')
                s.wait('vx1')
                s.op(s.activation(asm[:, h, 0:64], ps[:, hs, 0:64], AF.Copy,
                                  scale=0.125))
                s.wait('ss')
                s.op(s.copy(asmk[:, h, 0:64], ps[:, hs, 64:128]))
                s.wait('ss')
                s.op(s.copy(vev[:, h, :], ps[:, hs, 128:192]))

        vev_mark = ctx.count('ss')

        @ctx.sec('scalar')
        def _(s, t=t, vev_mark=vev_mark):
            s.wait('ss', vev_mark)
            s.wait('vx%d' % (t % 2))
            s.dma(W['v_pre'][t * 128:(t + 1) * 128, :],
                  vev[:].rearrange("p h d -> p (h d)"), sem='vx%d' % (t % 2))

        @ctx.sec('tensor')
        def _(te, t=t, vev_mark=vev_mark):
            te.wait('ss', vev_mark)
            te.wait('vs')
            for half in range(2):
                te.op(te.matmul(ps_s[:, half * 512:(half + 1) * 512], onesc[:],
                                vev[:].rearrange("p h d -> p (h d)")[:, half * 512:
                                                                    (half + 1) * 512],
                                start=True, stop=True))

        @ctx.sec('vector')
        def _(v, t=t):
            v.wait('ts')
            v.wait('tx%d' % (t % 2))
            v.op(v.tensor_copy(tsev[:, t % 2, :], ps_s[:]))

        @ctx.sec('scalar')
        def _(s, t=t):
            s.wait('vs')
            s.dma(W['tilesums'][t:t + 1, :], tsev[:, t % 2, :], sem='tx%d' % (t % 2))

        for h in range(16):
            sl = h % 2

            @ctx.sec('tensor')
            def _(te, t=t, h=h, sl=sl):
                te.wait('ss')
                te.op(te.transpose(ps_tp[:, sl, :], asm[:, h, :], ident16[:]))

            @ctx.sec('scalar')
            def _(s, t=t, h=h, sl=sl):
                s.wait('ts')
                s.wait('qx%d' % sl)
                s.op(s.copy(aev[:, sl, :], ps_tp[0:66, sl, :]))
                s.wait('ss')
                s.dma(W['q_pre'][h, :, t * 128:(t + 1) * 128], aev[:, sl, :],
                      sem='qx%d' % sl)
        for h in range(16):
            sl = h % 2

            @ctx.sec('tensor')
            def _(te, t=t, h=h, sl=sl):
                te.wait('ss')
                te.op(te.transpose(ps_tp[:, sl, :], asmk[:, h, :], ident16[:]))

            @ctx.sec('scalar')
            def _(s, t=t, h=h, sl=sl):
                s.wait('ts')
                s.wait('qx%d' % sl)
                s.op(s.copy(aev[:, sl, :], ps_tp[0:66, sl, :]))
                s.wait('ss')
                s.dma(W['k_pre'][h, :, t * 128:(t + 1) * 128], aev[:, sl, :],
                      sem='qx%d' % sl)
    return (ctx.count('qx0'), ctx.count('qx1'), ctx.count('vx0'), ctx.count('vx1'),
            ctx.count('tx0'), ctx.count('tx1'))


def emit_attna_attn(ctx, H, W, prefix="aa", win_seq=None, seq_glob=None):
    NT, T_LOC = H['NT'], H['T_LOC']
    ident16, identf = H['ident16'], H['identf']
    CH = T_LOC // 2
    WSEQ = win_seq
    WIN = WSEQ + CH
    QPB = CH // 512
    NQT8 = CH // 128
    NTG = seq_glob // 128
    NTB = NT // 2

    wr_sb = ctx.sbuf(prefix + "wr", [128, DK, D], BF16)
    br_sb = ctx.sbuf(prefix + "br", [1, D], BF16)
    ones1 = ctx.sbuf(prefix + "ones1", [1, 128], BF16)
    tri16 = ctx.sbuf(prefix + "tri16", [128, 128], BF16)
    msk = ctx.sbuf(prefix + "msk", [128, 4, 512], F32)

    @ctx.sec('gpsimd')
    def _(g):
        g.dma(wr_sb[:], W['wr'].rearrange("(c p) m -> p c m", p=128))
        g.dma(br_sb[:], W['br'])
        g.op(g.memset(ones1[:], 1.0))
        g.op(g.memset(tri16[:], 1.0))
        g.wait('gs')
        g.op(g.affine_select(tri16[:], tri16[:], compare_op=ALU.is_ge, fill=0.0,
                             base=0, pattern=[[-1, 128]], channel_multiplier=1))
        g.op(g.memset(msk[:], 1.0))
        g.wait('gs')
        for j in range(4):
            g.op(g.affine_select(msk[:, j, :], msk[:, j, :], compare_op=ALU.is_ge,
                                 fill=0.0, base=-128 * j - 1, pattern=[[1, 512]],
                                 channel_multiplier=-1))
    wload = ctx.count('dgp')
    cst_mark = ctx.count('gs')

    @ctx.sec('gpsimd')
    def _(g):
        g.dma(W['qaug_dram'][:], W['q_pre'][:])
    qcp_mark = ctx.count('dgp')

    ps_sc = ctx.psum(prefix + "ps_sc", [128, 4, 512], F32)
    ph0 = contextlib.ExitStack()
    suffv = ctx.sbuf(prefix + "suffv", [128, NT, 1152], BF16)
    v_own = ctx.sbuf(prefix + "v_own", [128, NT, 1152], BF16, es=ph0)
    msel_sb = ctx.sbuf(prefix + "msel", [128, 2, NTB], F32, es=ph0)
    tsg_sb = ctx.sbuf(prefix + "tsg", [128, 2, 1152], F32, es=ph0)
    brow = ctx.sbuf(prefix + "brow", [1, 2, 1152], BF16, es=ph0)
    ps_sf = ctx.psum(prefix + "ps_sf", [128, 3, 512], F32, es=ph0)

    @ctx.sec('gpsimd')
    def _(g):
        g.dma(v_own[:], W['v_aug'].rearrange("(n p) d -> p n d", p=128))
        g.dma(msel_sb[0:NTG, :, :], W['msel'].rearrange("b g j -> g b j"),
              allow_slow_non_contiguous=True)
        g.dma(tsg_sb[0:NTG, :, :], W['tsaug'].rearrange("b g d -> g b d"))
    sfl_mark = ctx.count('dgp')

    for b in range(2):
        for j in range(NTB):
            t = b * NTB + j
            sl = j % 2

            @ctx.sec('tensor')
            def _(te, b=b, j=j, sl=sl):
                te.wait('dgp', sfl_mark)
                te.wait('vs')
                for part in range(3):
                    w = 512 if part < 2 else 128
                    te.op(te.matmul(ps_sf[0:1, part, 0:w],
                                    msel_sb[0:NTG, b, j:j + 1],
                                    tsg_sb[0:NTG, b, part * 512:part * 512 + w],
                                    start=True, stop=True))

            @ctx.sec('vector')
            def _(v, sl=sl):
                v.wait('ts')
                v.op(v.tensor_copy(brow[:, sl, 0:512], ps_sf[0:1, 0, :]))
                v.op(v.tensor_copy(brow[:, sl, 512:1024], ps_sf[0:1, 1, :]))
                v.op(v.tensor_copy(brow[:, sl, 1024:1152], ps_sf[0:1, 2, 0:128]))
            brow_mark = ctx.count('vs')

            @ctx.sec('tensor')
            def _(te, t=t, sl=sl, brow_mark=brow_mark):
                te.wait('gs', cst_mark)
                te.wait('vs', brow_mark)
                te.wait('vs')
                for part in range(3):
                    w = 512 if part < 2 else 128
                    te.op(te.matmul(ps_sf[:, part, 0:w], tri16[:],
                                    v_own[:, t, part * 512:part * 512 + w],
                                    start=True, stop=False))
                    te.op(te.matmul(ps_sf[:, part, 0:w], ones1[:],
                                    brow[:, sl, part * 512:part * 512 + w],
                                    start=False, stop=True))

            @ctx.sec('vector')
            def _(v, t=t):
                v.wait('ts')
                v.op(v.tensor_copy(suffv[:, t, 0:512], ps_sf[:, 0, :]))
                v.op(v.tensor_copy(suffv[:, t, 512:1024], ps_sf[:, 1, :]))
                v.op(v.tensor_copy(suffv[:, t, 1024:1152], ps_sf[:, 2, 0:128]))
    suffv_mark = ctx.count('vs')
    ph0.close()

    q_hb = ctx.sbuf(prefix + "q_hb", [66, 2, CH], BF16)
    kwb = ctx.sbuf(prefix + "kwb", [66, 2, 2048], BF16)
    rm = ctx.sbuf(prefix + "rm", [128, NQT8, 17], F32)
    rmtok = ctx.sbuf(prefix + "rmtok", [128, NT, 16], F32)
    rmx16 = ctx.sbuf(prefix + "rmx16", [128, NQT8], BF16)
    rrow = ctx.sbuf(prefix + "rrow", [1, 2, 128], BF16)
    phA = contextlib.ExitStack()
    ps_ms = ctx.psum(prefix + "ps_ms", [128, 512], F32, es=phA)
    NB_A = WIN // 512
    NGA = (NB_A + 3) // 4

    for b in range(2):
        for h in range(16):
            qs = (b * 16 + h) % 2

            @ctx.sec('sync')
            def _(s, b=b, h=h, qs=qs):
                s.wait('ts')
                s.dma(q_hb[:, qs, :], W['q_pre'][h, :, b * CH:(b + 1) * CH],
                      sem='qh%d' % qs)

            for bg in range(0, NB_A, 4):
                nb = min(4, NB_A - bg)
                ks = (bg // 4) % 2

                @ctx.sec('sync')
                def _(s, b=b, h=h, bg=bg, nb=nb, ks=ks):
                    s.wait('ts')
                    s.dma(kwb[:, ks, 0:nb * 512],
                          W['k_win'][b, h, :, bg * 512:(bg + nb) * 512],
                          sem='kb%d' % ks)

                for qt in range(NQT8):
                    @ctx.sec('tensor')
                    def _(te, qt=qt, bg=bg, nb=nb, ks=ks, qs=qs):
                        te.wait('qh%d' % qs)
                        te.wait('kb%d' % ks)
                        te.wait('vs')
                        for k in range(nb):
                            te.op(te.matmul(ps_sc[:, k, :],
                                            q_hb[0:64, qs, qt * 128:(qt + 1) * 128],
                                            kwb[0:64, ks, k * 512:(k + 1) * 512],
                                            start=True, stop=True))

                    @ctx.sec('vector')
                    def _(v, qt=qt, bg=bg, nb=nb):
                        v.wait('ts')
                        v.op(v.reduce_max(rm[:, qt, bg // 4:bg // 4 + 1],
                                          ps_sc[:, 0:nb, :], axis=AX.XY))

            for qt in range(NQT8):
                @ctx.sec('vector')
                def _(v, b=b, h=h, qt=qt):
                    v.wait('vs')
                    v.op(v.reduce_max(rm[:, qt, 16:17], rm[:, qt, 0:NGA], axis=AX.X))
                    v.wait('vs')
                    v.op(v.tensor_scalar(rm[:, qt, 16:17], rm[:, qt, 16:17], 0.0,
                                         None, ALU.max))
                    v.wait('vs')
                    v.op(v.tensor_copy(rmtok[:, b * NTB + qt, h:h + 1],
                                       rm[:, qt, 16:17]))
                    v.op(v.tensor_copy(rmx16[:, qt:qt + 1], rm[:, qt, 16:17]))

            for qt in range(NQT8):
                sl = qt % 2

                @ctx.sec('tensor')
                def _(te, qt=qt, sl=sl):
                    te.wait('gs')
                    te.wait('vs')
                    te.wait('ss')
                    te.op(te.matmul(ps_ms[0:1, sl * 128:(sl + 1) * 128],
                                    rmx16[:, qt:qt + 1], ident16[:],
                                    start=True, stop=True))

                @ctx.sec('scalar')
                def _(s, b=b, h=h, qt=qt, sl=sl):
                    s.wait('ts')
                    s.wait('qw%d' % sl)
                    s.op(s.copy(rrow[:, sl, :], ps_ms[0:1, sl * 128:(sl + 1) * 128]))
                    s.wait('ss')
                    s.dma(W['qaug_dram'][h, 64:65,
                                         b * CH + qt * 128:b * CH + (qt + 1) * 128],
                          rrow[:, sl, :], sem='qw%d' % sl)
    passa_mark = (ctx.count('qw0'), ctx.count('qw1'))
    phA.close()

    att16 = ctx.sbuf(prefix + "att16", [128, DK, T_LOC], BF16)
    qab = ctx.sbuf(prefix + "qab", [66, 2, 512], BF16)
    vwb = ctx.psum if False else ctx.sbuf(prefix + "vwb", [128, 2, 8, 65], BF16)
    pb16 = ctx.sbuf(prefix + "pb16", [128, 4, 512], BF16)
    pvev = ctx.sbuf(prefix + "pvev", [65, 512], F32)
    outtok = ctx.sbuf(prefix + "outtok", [128, 4, 16, 65], F32)
    emt = ctx.sbuf(prefix + "emt", [128, 4, 16], F32)
    scr = ctx.sbuf(prefix + "scr", [128, 16, 64], F32)
    of16 = ctx.sbuf(prefix + "of16", [128, 4, 16, 64], BF16)
    phB = contextlib.ExitStack()
    ps_pv = ctx.psum(prefix + "ps_pv", [65, 512], F32, es=phB)
    ps_tr = ctx.psum(prefix + "ps_tr", [128, 130], F32, es=phB)
    ps_at = ctx.psum(prefix + "ps_at", [128, 2, 128], BF16, es=phB)

    for b in range(2):
        for qt in range(QPB):
            nsc = (WSEQ + qt * 512 + 512) // 128
            for h in range(16):
                qs = h % 2

                @ctx.sec('sync')
                def _(s, b=b, qt=qt, h=h, qs=qs):
                    s.wait('qw0', passa_mark[0])
                    s.wait('qw1', passa_mark[1])
                    s.wait('ts')
                    s.dma(qab[:, qs, :],
                          W['qaug_dram'][h, :, b * CH + qt * 512:b * CH + (qt + 1) * 512],
                          sem='qh%d' % qs)

                for scq in range(0, nsc, 4):
                    ks = (scq // 8) % 2
                    if scq % 8 == 0:
                        ldw_ = min(1024, WIN - scq * 128)

                        @ctx.sec('sync')
                        def _(s, b=b, h=h, scq=scq, ldw_=ldw_, ks=ks):
                            s.wait('ts')
                            s.dma(kwb[:, ks, 0:ldw_],
                                  W['k_win'][b, h, :, scq * 128:scq * 128 + ldw_],
                                  sem='kb%d' % ks)
                            s.dma(vwb[:, ks, 0:ldw_ // 128, :],
                                  W['v_win'][b, scq * 128:scq * 128 + ldw_, h, :]
                                  .rearrange("(k p) c -> p k c", p=128),
                                  sem='kb%d' % ks)

                    last = (scq + 4 >= nsc)

                    @ctx.sec('tensor')
                    def _(te, qt=qt, scq=scq, ks=ks, qs=qs):
                        te.wait('kb%d' % ks)
                        te.wait('qh%d' % qs)
                        te.wait('ss')
                        for k in range(4):
                            ksl = (scq + k) % 8
                            te.op(te.matmul(ps_sc[:, k, :],
                                            kwb[:, ks, ksl * 128:(ksl + 1) * 128],
                                            qab[:, qs, :], start=True, stop=True))

                    @ctx.sec('scalar')
                    def _(s, scq=scq):
                        s.wait('ts')
                        s.wait('vs')
                        s.op(s.activation(pb16[:, :, :], ps_sc[:, :, :], AF.Exp))

                    if last:
                        @ctx.sec('vector')
                        def _(v):
                            v.wait('ss')
                            v.wait('gs', cst_mark)
                            v.op(v.tensor_tensor(pb16[:, :, :], pb16[:, :, :],
                                                 msk[:, :, :], ALU.mult))

                    @ctx.sec('tensor')
                    def _(te, scq=scq, nsc=nsc, last=last, ks=ks):
                        if last:
                            te.wait('vs')
                        else:
                            te.wait('ss')
                        if scq == 0:
                            te.wait('ss')
                        for k in range(4):
                            ksl = (scq + k) % 8
                            te.op(te.matmul(ps_pv[:, :], vwb[:, ks, ksl, :],
                                            pb16[:, k, :],
                                            start=(scq + k == 0),
                                            stop=(scq + k == nsc - 1)))

                @ctx.sec('scalar')
                def _(s, h=h):
                    s.wait('ts')
                    s.wait('vs')
                    s.op(s.copy(pvev[:], ps_pv[:]))
                pv_mark = ctx.count('ss')

                for sub in range(4):
                    @ctx.sec('tensor')
                    def _(te, sub=sub, pv_mark=pv_mark):
                        te.wait('ss', pv_mark)
                        te.wait('vs')
                        te.op(te.transpose(ps_tr[:, 0:65],
                                           pvev[:, sub * 128:(sub + 1) * 128],
                                           identf[0:65, 0:65]))

                    @ctx.sec('vector')
                    def _(v, sub=sub, h=h):
                        v.wait('ts')
                        v.op(v.tensor_copy(outtok[:, sub, h, :], ps_tr[:, 0:65]))
            otok_mark = ctx.count('vs')

            for sub in range(4):
                t = b * NTB + qt * 4 + sub

                @ctx.sec('scalar')
                def _(s, t=t, sub=sub, otok_mark=otok_mark):
                    s.wait('vs', otok_mark)
                    s.op(s.activation(emt[:, sub, :], rmtok[:, t, :], AF.Exp,
                                      scale=-1.0))

                @ctx.sec('vector')
                def _(v, t=t, sub=sub, b=b, qt=qt):
                    v.wait('ss')
                    v.wait('vs', suffv_mark)
                    sv3 = suffv[:, t, 0:1024].rearrange("p (h d) -> p h d", h=16)
                    v.op(v.tensor_tensor(scr[:], sv3,
                                         emt[:, sub, :].unsqueeze(2)
                                         .broadcast_to([128, 16, 64]), ALU.mult))
                    v.wait('vs')
                    v.op(v.tensor_tensor(outtok[:, sub, :, 0:64],
                                         outtok[:, sub, :, 0:64], scr[:], ALU.add))
                    v.wait('vs')
                    v.op(v.scalar_tensor_tensor(emt[:, sub, :], emt[:, sub, :],
                                                suffv[:, t, 1024:1025],
                                                outtok[:, sub, :, 64], ALU.mult,
                                                ALU.add))
                    v.wait('vs')
                    v.op(v.reciprocal(emt[:, sub, :], emt[:, sub, :]))
                    v.wait('vs')
                    v.op(v.tensor_tensor(of16[:, sub, :, :], outtok[:, sub, :, 0:64],
                                         emt[:, sub, :].unsqueeze(2)
                                         .broadcast_to([128, 16, 64]), ALU.mult))
            fnorm_mark = ctx.count('vs')

            for sub in range(4):
                t = b * NTB + qt * 4 + sub
                for c in range(DK):
                    sl = c % 2

                    @ctx.sec('tensor')
                    def _(te, t=t, sub=sub, c=c, sl=sl, fnorm_mark=fnorm_mark):
                        te.wait('vs', fnorm_mark)
                        te.wait('ss')
                        te.op(te.transpose(ps_at[:, sl, :],
                                           of16[:, sub, :, :]
                                           .rearrange("p h d -> p (h d)")
                                           [:, c * 128:(c + 1) * 128], ident16[:]))

                    @ctx.sec('scalar')
                    def _(s, t=t, c=c, sl=sl):
                        s.wait('ts')
                        s.op(s.copy(att16[:, c, t * 128:(t + 1) * 128],
                                    ps_at[:, sl, :]))
    att_mark = ctx.count('ss')
    phB.close()

    xb = ctx.sbuf(prefix + "xb", [128, 2, D], F32)
    ps_o = ctx.psum(prefix + "ps_o", [128, 2, 512], F32)
    for t in range(NT):
        bsl = t % 2

        @ctx.sec('sync')
        def _(s, t=t, bsl=bsl):
            s.wait('vs')
            s.wait('xo%d' % bsl)
            s.dma(xb[:, bsl, :], W['x_in'][t * 128:(t + 1) * 128, :],
                  sem='xl%d' % bsl)

        @ctx.sec('tensor')
        def _(te, t=t, bsl=bsl):
            te.wait('ss', att_mark)
            te.wait('dgp', wload)
            te.wait('vs')
            for half in range(2):
                for c in range(DK):
                    te.op(te.matmul(ps_o[:, half, :],
                                    att16[:, c, t * 128:(t + 1) * 128],
                                    wr_sb[:, c, half * 512:(half + 1) * 512],
                                    start=(c == 0), stop=False))
                te.op(te.matmul(ps_o[:, half, :], ones1[:],
                                br_sb[:, half * 512:(half + 1) * 512],
                                start=False, stop=True))

        @ctx.sec('vector')
        def _(v, t=t, bsl=bsl):
            v.wait('ts')
            v.wait('xl%d' % bsl)
            v.op(v.tensor_add(xb[:, bsl, 0:512], xb[:, bsl, 0:512], ps_o[:, 0, :]))
            v.wait('vs')
            v.op(v.tensor_add(xb[:, bsl, 512:1024], xb[:, bsl, 512:1024], ps_o[:, 1, :]))

        @ctx.sec('scalar')
        def _(s, t=t, bsl=bsl):
            s.wait('vs')
            s.dma(W['x_out'][t * 128:(t + 1) * 128, :], xb[:, bsl, :],
                  sem='xo%d' % bsl)
    return (ctx.count('xo0'), ctx.count('xo1'))



def emit_rms_to_fm(ctx, W, H, rms16_fm, prefix, es=None, x_tm_out=None):
    NT = H['NT']
    ident16 = H['ident16']
    s_loc = ctx.sbuf(prefix + "s_loc", [128, NT], F32)
    ph = contextlib.ExitStack()
    x_tm = ctx.sbuf(prefix + "x_tm", [128, NT, D], F32, es=ph)
    rtb = ctx.sbuf(prefix + "rtb", [128, D], F32, es=ph)
    sq = ctx.sbuf(prefix + "sq", [128, D], F32, es=ph)
    st = ctx.sbuf(prefix + "st", [128, 4], F32, es=ph)
    r16 = ctx.sbuf(prefix + "r16", [128, 2, D], BF16, es=ph)
    tp_ps = ctx.psum(prefix + "tp_ps", [128, 1024], BF16, es=ph)
    xm = emit_load_x(ctx, x_tm, W['x_in'])

    @ctx.sec('gpsimd')
    def _(g):
        g.dma(rtb[:], W['rt_row'].broadcast_to([128, D]))
    rt_mark = ctx.count('dgp')

    ln16d = ctx.sbuf(prefix + "ln16d", [128, 2, D], BF16, es=ph)
    emit_ln(ctx, x_tm, ln16d, NT, x_ready=[('dsy', xm)], prefix=prefix + "n",
            es=ph, lnxf=x_tm)

    tp_marks = {}
    for t in range(NT):
        @ctx.sec('scalar')
        def _(s, t=t):
            s.wait('dsy', xm)
            s.wait('vs')
            s.op(s.activation(sq[:], x_tm[:, t, :], AF.Square, accum_out=st[:, 0:1]))

        @ctx.sec('vector')
        def _(v, t=t):
            v.wait('ss')
            v.wait('dgp', rt_mark)
            v.op(v.tensor_tensor(sq[:], x_tm[:, t, :], rtb[:], ALU.mult))
            v.wait('vs')
            v.op(v.reduce_sum(s_loc[:, t:t + 1], sq[:], axis=AX.X))
            v.wait('vs')

        @ctx.sec('scalar')
        def _(s, t=t):
            s.wait('vs')
            s.op(s.activation(st[:, 1:2], st[:, 0:1], AF.Sqrt))

        @ctx.sec('vector')
        def _(v, t=t):
            v.wait('ss')
            v.op(v.tensor_scalar(st[:, 2:3], st[:, 1:2], 1e-12, None, ALU.max))
            v.wait('vs')
            v.op(v.reciprocal(st[:, 3:4], st[:, 2:3]))
            v.wait('vs')
            if t >= 2 and (t - 2) in tp_marks:
                v.wait('ts', tp_marks[t - 2])
            v.op(v.tensor_scalar(r16[:, t % 2, :], x_tm[:, t, :], st[:, 3:4], 32.0,
                                 ALU.mult, ALU.mult))
        rmk = ctx.count('vs')

        for c in range(DK):
            slot = (t * DK + c) % 4

            @ctx.sec('tensor')
            def _(te, t=t, c=c, slot=slot, rmk=rmk):
                te.wait('gs')
                te.wait('vs', rmk)
                if (t * DK + c) >= 4:
                    te.wait('ss')
                te.op(te.transpose(tp_ps[:, slot * 128:(slot + 1) * 128],
                                   r16[:, t % 2, c * 128:(c + 1) * 128], ident16[:]))

            @ctx.sec('scalar')
            def _(s, t=t, c=c, slot=slot):
                s.wait('ts')
                s.op(s.copy(rms16_fm[:, c, t * 128:(t + 1) * 128],
                            tp_ps[:, slot * 128:(slot + 1) * 128]))
        tp_marks[t] = ctx.count('ts')
    done = ctx.count('ss')
    ph.close()
    return done, s_loc


def emit_l1ffn(ctx, H, W, prefix="lf", seq_glob=None):
    NT, T_LOC = H['NT'], H['T_LOC']
    ident16 = H['ident16']
    CH = T_LOC // 2
    NTB = NT // 2
    EPS = 0.1
    C1E = math.log(8.0)
    PP = seq_glob // 64
    RPB = NT

    rms16 = ctx.sbuf(prefix + "rms16", [128, DK, T_LOC], BF16)
    rms_mark, s_loc = emit_rms_to_fm(ctx, W, H, rms16, prefix + "r")

    @ctx.sec('scalar')
    def _(s):
        s.wait('vs')
        for t in range(NT):
            b = t // NTB
            tb = t % NTB
            s.dma(W['s_ag_in'][b * RPB + 2 * tb:b * RPB + 2 * tb + 2, :],
                  s_loc[:, t:t + 1], sem='sx')

    @ctx.sec('gpsimd')
    def _(g):
        g.wait('sx')
        g.op(g.collective_compute(
            "AllGather", ALU.bypass, replica_groups=[list(range(NCORE))],
            ins=[W['s_ag_in']], outs=[W['s_ag_out']]), sem='cc')
    ag_mark = ctx.count('cc')

    sp = ctx.sbuf(prefix + "sp", [128, 2, 64], F32)
    nb_ = ctx.sbuf(prefix + "nb", [128, 2, 64], F32)
    wz = ctx.sbuf(prefix + "wz", [128, 2, 64], F32)
    ee = ctx.sbuf(prefix + "ee", [128, 2, 64], F32)
    msks = ctx.sbuf(prefix + "msks", [128, 2, 64], F32)
    red2 = ctx.sbuf(prefix + "red2", [128, 4], F32)
    red16 = ctx.sbuf(prefix + "red16", [128, 2], BF16)
    grow = ctx.sbuf(prefix + "grow", [1, 16], F32)
    iw = ctx.sbuf(prefix + "iw", [1, 16], I32)
    iwf = iw[:].bitcast(F32)
    onesf = ctx.sbuf(prefix + "onesf", [1, 128], F32)
    onesfc = ctx.sbuf(prefix + "onesfc", [128, 1], F32)
    id2f = ctx.sbuf(prefix + "id2f", [2, 2], F32)
    strict_lo = ctx.sbuf(prefix + "strict_lo", [128, 128], BF16)
    phD = contextlib.ExitStack()
    ps_d = ctx.psum(prefix + "ps_d", [128, 512], F32, es=phD)
    ps_e = ctx.psum(prefix + "ps_e", [128, 16], F32, es=phD)

    @ctx.sec('gpsimd')
    def _(g):
        g.op(g.memset(onesf[:], 1.0))
        g.op(g.memset(onesfc[:], 1.0))
        g.op(g.memset(id2f[:], 0.0))
        g.op(g.memset(strict_lo[:], 1.0))
        g.wait('gs')
        g.op(g.affine_select(id2f[:], id2f[:], compare_op=ALU.not_equal, fill=1.0,
                             base=0, pattern=[[-1, 2]], channel_multiplier=1))
        g.op(g.affine_select(strict_lo[:], strict_lo[:], compare_op=ALU.is_ge,
                             fill=0.0, base=-1, pattern=[[1, 128]],
                             channel_multiplier=-1))
    cst_mark = ctx.count('gs')

    @ctx.sec('sync')
    def _(s):
        s.wait('cc', ag_mark)
        for b in range(2):
            for k in range(NCORE):
                c = k if b == 0 else (7 - k)
                s.dma(sp[k * RPB:(k + 1) * RPB, b, :],
                      W['s_ag_out'][c * 2 * RPB + b * RPB:c * 2 * RPB + (b + 1) * RPB, :],
                      sem='sg')

    @ctx.sec('vector')
    def _(v):
        v.wait('sg')
        v.op(v.tensor_scalar(sp[0:PP], sp[0:PP], 1.0 / EPS, None, ALU.mult))
        v.wait('vs')
        v.op(v.tensor_scalar(nb_[0:PP], sp[0:PP], 0.0, None, ALU.max))

    for it in range(50):
        @ctx.sec('vector')
        def _(v, it=it):
            v.wait('vs')
            v.wait('ss')
            v.wait('ts')
            v.op(v.tensor_sub(wz[0:PP], sp[0:PP], nb_[0:PP]))
            v.wait('vs')
            v.op(v.reduce_max(red2[0:PP, 0:2], wz[0:PP], axis=AX.X))
            v.wait('vs')
            v.op(v.tensor_copy(red16[0:PP], red2[0:PP, 0:2]))

        @ctx.sec('tensor')
        def _(te, it=it):
            te.wait('gs', cst_mark)
            te.wait('vs')
            te.op(te.matmul(ps_d[0:2, 0:PP], red16[0:PP], ident16[0:PP, 0:PP],
                            start=True, stop=True))

        @ctx.sec('vector')
        def _(v, it=it):
            v.wait('ts')
            v.op(v.reduce_max(red2[0:2, 2:3], ps_d[0:2, 0:PP], axis=AX.X))
            v.wait('vs')

        @ctx.sec('tensor')
        def _(te, it=it):
            te.wait('vs')
            te.op(te.matmul(ps_d[0:1, 128:130], red2[0:2, 2:3], id2f[:],
                            start=True, stop=True))

        @ctx.sec('vector')
        def _(v, it=it):
            v.wait('ts')
            v.op(v.tensor_copy(grow[:, 0:2], ps_d[0:1, 128:130]))

        @ctx.sec('tensor')
        def _(te, it=it):
            te.wait('vs')
            te.op(te.matmul(ps_d[:, 130:132], onesf[:], grow[:, 0:2],
                            start=True, stop=True))

        @ctx.sec('vector')
        def _(v, it=it):
            v.wait('ts')
            v.op(v.tensor_tensor(ee[0:PP], wz[0:PP],
                                 ps_d[0:PP, 130:132].unsqueeze(2)
                                 .broadcast_to([PP, 2, 64]), ALU.subtract))

        @ctx.sec('scalar')
        def _(s, it=it):
            s.wait('vs')
            s.op(s.activation(ee[0:PP], ee[0:PP], AF.Exp))

        @ctx.sec('vector')
        def _(v, it=it):
            v.wait('ss')
            v.op(v.reduce_sum(red2[0:PP, 0:2], ee[0:PP], axis=AX.X))
            v.wait('vs')

        @ctx.sec('tensor')
        def _(te, it=it):
            te.wait('vs')
            te.op(te.matmul(ps_d[0:1, 132:134], onesfc[0:PP], red2[0:PP, 0:2],
                            start=True, stop=True))

        @ctx.sec('scalar')
        def _(s, it=it):
            s.wait('ts')
            s.op(s.activation(grow[:, 2:4], ps_d[0:1, 132:134], AF.Ln))

        @ctx.sec('vector')
        def _(v, it=it):
            v.wait('ss')
            v.op(v.tensor_add(grow[:, 4:6], grow[:, 2:4], grow[:, 0:2]))
            v.wait('vs')
            v.op(v.tensor_scalar(grow[:, 4:6], grow[:, 4:6], -1.0, C1E,
                                 ALU.mult, ALU.add))
        ap_mark = ctx.count('vs')

        @ctx.sec('tensor')
        def _(te, it=it, ap_mark=ap_mark):
            te.wait('vs', ap_mark)
            te.op(te.matmul(ps_e[:, 0:2], onesf[:], grow[:, 4:6],
                            start=True, stop=True))

        @ctx.sec('vector')
        def _(v, it=it):
            v.wait('ts')
            v.op(v.tensor_tensor(nb_[0:PP], sp[0:PP],
                                 ps_e[0:PP, 0:2].unsqueeze(2)
                                 .broadcast_to([PP, 2, 64]), ALU.add))
            v.wait('vs')
            v.op(v.tensor_scalar(nb_[0:PP], nb_[0:PP], 0.0, None, ALU.max))

    @ctx.sec('vector')
    def _(v):
        v.wait('vs')
        v.op(v.tensor_tensor(wz[0:PP], sp[0:PP],
                             ps_e[0:PP, 0:2].unsqueeze(2)
                             .broadcast_to([PP, 2, 64]), ALU.add))
        v.wait('vs')
        v.op(v.tensor_scalar(wz[0:PP], wz[0:PP], 0.0, None, ALU.min))

    @ctx.sec('scalar')
    def _(s):
        s.wait('vs')
        s.op(s.activation(ee[0:PP], wz[0:PP], AF.Exp))
    e_mark = ctx.count('ss')

    @ctx.sec('gpsimd')
    def _(g):
        g.wait('ss', e_mark)
        g.op(g.memset(iw[:, 0:2], 0))
        g.wait('gs')
        g.op(g.memset(iwf[:, 2:4], 2.0))

    for it in range(33):
        @ctx.sec('vector')
        def _(v, it=it):
            v.wait('gs')
            v.wait('ts')
            v.wait('vs')
            v.op(v.tensor_tensor(iw[:, 4:6], iw[:, 0:2], iw[:, 2:4], ALU.add))
            v.wait('vs')
            v.op(v.tensor_scalar(iw[:, 4:6], iw[:, 4:6], 1, None,
                                 ALU.arith_shift_right))
            v.wait('vs')
            v.op(v.tensor_copy(grow[:, 8:10], iwf[:, 4:6]))
        mid_mark = ctx.count('vs')

        @ctx.sec('tensor')
        def _(te, it=it, mid_mark=mid_mark):
            te.wait('vs', mid_mark)
            te.op(te.matmul(ps_e[:, 2:4], onesf[:], grow[:, 8:10],
                            start=True, stop=True))

        @ctx.sec('vector')
        def _(v, it=it):
            v.wait('ts')
            v.op(v.tensor_tensor(msks[0:PP], ee[0:PP], ps_e[0:PP, 2:4]
                                 .unsqueeze(2)
                                 .broadcast_to([PP, 2, 64]), ALU.is_gt))
            v.wait('vs')
            v.op(v.reduce_sum(red2[0:PP, 0:2], msks[0:PP], axis=AX.X))
            v.wait('vs')

        @ctx.sec('tensor')
        def _(te, it=it):
            te.wait('vs')
            te.op(te.matmul(ps_d[0:1, 134:136], onesfc[0:PP], red2[0:PP, 0:2],
                            start=True, stop=True))

        @ctx.sec('vector')
        def _(v, it=it):
            v.wait('ts')
            v.op(v.tensor_scalar(grow[:, 10:12], ps_d[0:1, 134:136], 1023.5, None,
                                 ALU.is_ge))
            v.wait('vs')
            v.op(v.tensor_copy(iw[:, 6:8], grow[:, 10:12]))
            v.wait('vs')
            v.op(v.tensor_tensor(iw[:, 8:10], iw[:, 4:6], iw[:, 0:2], ALU.subtract))
            v.wait('vs')
            v.op(v.tensor_tensor(iw[:, 8:10], iw[:, 8:10], iw[:, 6:8], ALU.mult))
            v.wait('vs')
            v.op(v.tensor_tensor(iw[:, 0:2], iw[:, 0:2], iw[:, 8:10], ALU.add))
            v.wait('vs')
            v.op(v.tensor_scalar(iw[:, 10:12], iw[:, 6:8], -1, 1, ALU.mult, ALU.add))
            v.wait('vs')
            v.op(v.tensor_tensor(iw[:, 12:14], iw[:, 2:4], iw[:, 4:6], ALU.subtract))
            v.wait('vs')
            v.op(v.tensor_tensor(iw[:, 12:14], iw[:, 12:14], iw[:, 10:12], ALU.mult))
            v.wait('vs')
            v.op(v.tensor_tensor(iw[:, 2:4], iw[:, 2:4], iw[:, 12:14], ALU.subtract))

    @ctx.sec('vector')
    def _(v):
        v.wait('vs')
        v.op(v.tensor_copy(grow[:, 12:14], iwf[:, 2:4]))
    tau_mark = ctx.count('vs')

    @ctx.sec('tensor')
    def _(te, tau_mark=tau_mark):
        te.wait('vs', tau_mark)
        te.op(te.matmul(ps_e[:, 4:6], onesf[:], grow[:, 12:14], start=True, stop=True))

    @ctx.sec('vector')
    def _(v):
        v.wait('ts')
        taub = ps_e[0:PP, 4:6].unsqueeze(2).broadcast_to([PP, 2, 64])
        v.op(v.tensor_tensor(msks[0:PP], ee[0:PP], taub, ALU.is_gt))
        v.wait('vs')
        v.op(v.tensor_tensor(wz[0:PP], ee[0:PP], taub, ALU.is_equal))
        v.wait('vs')
        v.op(v.reduce_sum(red2[0:PP, 0:2], msks[0:PP], axis=AX.X))
        v.wait('vs')

    @ctx.sec('tensor')
    def _(te):
        te.wait('vs')
        te.op(te.matmul(ps_d[0:1, 136:138], onesfc[0:PP], red2[0:PP, 0:2],
                        start=True, stop=True))

    @ctx.sec('vector')
    def _(v):
        v.wait('ts')
        v.op(v.tensor_scalar(grow[:, 14:16], ps_d[0:1, 136:138], -1.0, 1024.0,
                             ALU.mult, ALU.add))
    need_mark = ctx.count('vs')

    @ctx.sec('tensor')
    def _(te, need_mark=need_mark):
        te.wait('vs', need_mark)
        te.op(te.matmul(ps_e[:, 6:8], onesf[:], grow[:, 14:16], start=True, stop=True))

    onesb = ctx.sbuf(prefix + "onesb", [128, 64], F32)
    tot16 = ctx.sbuf(prefix + "tot16", [128, 2], BF16)

    @ctx.sec('gpsimd')
    def _(g):
        g.op(g.memset(onesb[:], 1.0))

    @ctx.sec('vector')
    def _(v):
        v.wait('ts')
        v.wait('gs')
        for b in range(2):
            v.op(v.tensor_tensor_scan(nb_[0:PP, b, :], onesb[0:PP], wz[0:PP, b, :],
                                      0.0, ALU.mult, ALU.add))
            v.wait('vs')
        v.op(v.tensor_copy(tot16[0:PP, 0:1], nb_[0:PP, 0, 63:64]))
        v.op(v.tensor_copy(tot16[0:PP, 1:2], nb_[0:PP, 1, 63:64]))
    tot_mark = ctx.count('vs')

    @ctx.sec('tensor')
    def _(te, tot_mark=tot_mark):
        te.wait('vs', tot_mark)
        te.op(te.matmul(ps_e[0:PP, 8:10], strict_lo[0:PP, 0:PP], tot16[0:PP],
                        start=True, stop=True))

    @ctx.sec('vector')
    def _(v):
        v.wait('ts')
        v.op(v.tensor_sub(sp[0:PP], nb_[0:PP], wz[0:PP]))
        v.wait('vs')
        v.op(v.tensor_tensor(sp[0:PP], sp[0:PP], ps_e[0:PP, 8:10]
                             .unsqueeze(2).broadcast_to([PP, 2, 64]),
                             ALU.add))
        v.wait('vs')
        v.op(v.tensor_tensor(sp[0:PP], sp[0:PP], ps_e[0:PP, 6:8]
                             .unsqueeze(2).broadcast_to([PP, 2, 64]),
                             ALU.is_lt))
        v.wait('vs')
        v.op(v.tensor_tensor(wz[0:PP], wz[0:PP], sp[0:PP], ALU.mult))
        v.wait('vs')
        v.op(v.tensor_add(msks[0:PP], msks[0:PP], wz[0:PP]))
        v.wait('vs')
        v.op(v.tensor_scalar(sp[0:PP], ee[0:PP], -1.0, 1.0, ALU.mult, ALU.add))
        v.wait('vs')
        v.op(v.tensor_add(sp[0:PP], ee[0:PP], sp[0:PP]))
        v.wait('vs')
        v.op(v.tensor_tensor(ee[0:PP], msks[0:PP], sp[0:PP], ALU.mult))
    w_mark = ctx.count('vs')

    @ctx.sec('scalar')
    def _(s, w_mark=w_mark):
        s.wait('vs', w_mark)
        s.dma(W['w_dram'].rearrange("b p f -> p b f"), ee[0:PP], sem='wd')

    from concourse.bass_core_rust import IndirectOffsetOnAxis
    widx_sb = ctx.sbuf(prefix + "widx", [128, NT], mybir.dt.uint32)
    w_own = ctx.sbuf(prefix + "w_own", [128, NT], F32)

    @ctx.sec('sync')
    def _(s):
        s.dma(widx_sb[:], W['widx'].rearrange("(n p) c -> p (n c)", p=128), sem='wi',
              allow_slow_non_contiguous=True)

    @ctx.sec('gpsimd')
    def _(g):
        g.wait('wd')
        g.wait('wi')
        for t in range(NT):
            g.op(g.indirect_dma_start(
                w_own[:, t:t + 1], None,
                W['w_dram'].rearrange("b p f -> (b p f)").unsqueeze(1),
                IndirectOffsetOnAxis(widx_sb[:, t:t + 1], 0)), sem='gw', k=16)
    gw_mark = ctx.count('gw')

    phD.close()
    phL = contextlib.ExitStack()
    w1l_sb = ctx.sbuf(prefix + "w1l", [128, DK, 512], BF16, es=phL)
    w2l_sb = ctx.sbuf(prefix + "w2l", [128, 4, D], BF16, es=phL)
    b1l_sb = ctx.sbuf(prefix + "b1l", [1, 512], BF16, es=phL)
    b2l_sb = ctx.sbuf(prefix + "b2l", [1, D], BF16, es=phL)
    glcol = ctx.sbuf(prefix + "glcol", [128, DK], F32, es=phL)
    hl16 = ctx.sbuf(prefix + "hl16", [128, 4, T_LOC], BF16, es=phL)
    onesr16 = ctx.sbuf(prefix + "onesr16", [1, 512], BF16, es=phL)
    loev = ctx.sbuf(prefix + "loev", [128, 2, D], F32, es=phL)
    ps_w = ctx.psum(prefix + "ps_w", [128, 4, 512], F32, es=phL)

    @ctx.sec('gpsimd')
    def _(g):
        g.dma(w1l_sb[:], W['w1l'].rearrange("(c p) h -> p c h", p=128))
        g.dma(w2l_sb[:], W['w2l'].rearrange("(c p) h -> p c h", p=128))
        g.dma(b1l_sb[:], W['b1l'])
        g.dma(b2l_sb[:], W['b2l'])
        g.dma(glcol[:], W['gl'].rearrange("(c p) one -> p (c one)", p=128),
              allow_slow_non_contiguous=True)
        g.op(g.memset(onesr16[:], 1.0))
    wl_mark = ctx.count('dgp')

    @ctx.sec('vector')
    def _(v):
        v.wait('dgp', wl_mark)
        for c in range(DK):
            v.op(v.tensor_scalar(w1l_sb[:, c, :], w1l_sb[:, c, :], glcol[:, c:c + 1],
                                 None, ALU.mult))
            v.wait('vs')
    w1l_mark = ctx.count('vs')

    NQ = T_LOC // 512
    for m in range(4):
        for j in range(NQ):
            sl = (m * NQ + j) % 2

            @ctx.sec('tensor')
            def _(te, m=m, j=j, sl=sl):
                te.wait('vs', w1l_mark)
                te.wait('ss', rms_mark)
                te.wait('gs')
                te.wait('ss')
                for c in range(DK):
                    te.op(te.matmul(ps_w[:, sl, :], w1l_sb[:, c, m * 128:(m + 1) * 128],
                                    rms16[:, c, j * 512:(j + 1) * 512],
                                    start=(c == 0), stop=False))
                te.op(te.matmul(ps_w[:, sl, :], b1l_sb[:, m * 128:(m + 1) * 128],
                                onesr16[:], start=False, stop=True))

            @ctx.sec('scalar')
            def _(s, m=m, j=j, sl=sl):
                s.wait('ts')
                s.op(s.activation(hl16[:, m, j * 512:(j + 1) * 512], ps_w[:, sl, :],
                                  AF.Gelu))
    hl_mark = ctx.count('ss')

    for t in range(NT):
        sl = t % 2

        @ctx.sec('tensor')
        def _(te, t=t, sl=sl):
            te.wait('ss', hl_mark)
            te.wait('vs')
            for half in range(2):
                for kc in range(4):
                    te.op(te.matmul(ps_w[:, 2 + half, :],
                                    hl16[:, kc, t * 128:(t + 1) * 128],
                                    w2l_sb[:, kc, half * 512:(half + 1) * 512],
                                    start=(kc == 0), stop=False))
                te.op(te.matmul(ps_w[:, 2 + half, :], onesr16[:, 0:128],
                                b2l_sb[:, half * 512:(half + 1) * 512],
                                start=False, stop=True))
        @ctx.sec('vector')
        def _(v, t=t, sl=sl):
            v.wait('ts')
            v.wait('lo%d' % sl)
            v.op(v.tensor_copy(loev[:, sl, 0:512], ps_w[:, 2, :]))
            v.wait('vs')
            v.op(v.tensor_copy(loev[:, sl, 512:1024], ps_w[:, 3, :]))

        @ctx.sec('scalar')
        def _(s, t=t, sl=sl):
            s.wait('vs')
            s.dma(W['lightout'][t * 128:(t + 1) * 128, :], loev[:, sl, :],
                  sem='lo%d' % sl)
    lo_marks = (ctx.count('lo0'), ctx.count('lo1'))
    phL.close()

    HH = 4096
    NHH = HH // 128
    phH = contextlib.ExitStack()
    w1h_sb = ctx.sbuf(prefix + "w1h", [128, DK, HH], BF16, es=phH)
    b1h_sb = ctx.sbuf(prefix + "b1h", [1, HH], BF16, es=phH)
    b2h_sb = ctx.sbuf(prefix + "b2h", [1, D], BF16, es=phH)
    ghcol = ctx.sbuf(prefix + "ghcol", [128, DK], F32, es=phH)
    hh16 = ctx.sbuf(prefix + "hh16", [128, NHH, 512], BF16, es=phH)
    w2hb = ctx.sbuf(prefix + "w2hb", [128, 2, D], BF16, es=phH)
    onesh16 = ctx.sbuf(prefix + "onesh16", [1, 512], BF16, es=phH)
    xlb = ctx.sbuf(prefix + "xlb", [128, 2, 2, D], F32, es=phH)
    fin = ctx.sbuf(prefix + "fin", [128, 2, D], F32, es=phH)
    ps_h = ctx.psum(prefix + "ps_h", [128, 8, 512], F32, es=phH)

    @ctx.sec('gpsimd')
    def _(g):
        g.wait('ts')
        g.wait('vs')
        g.wait('ss')
        g.wait('lo0')
        g.wait('lo1')
        g.dma(w1h_sb[:], W['w1h'].rearrange("(c p) h -> p c h", p=128))
        g.dma(b1h_sb[:], W['b1h'])
        g.dma(b2h_sb[:], W['b2h'])
        g.dma(ghcol[:], W['gh'].rearrange("(c p) one -> p (c one)", p=128),
              allow_slow_non_contiguous=True)
        g.op(g.memset(onesh16[:], 1.0))
    wh_mark = ctx.count('dgp')

    @ctx.sec('vector')
    def _(v):
        v.wait('dgp', wh_mark)
        for c in range(DK):
            v.op(v.tensor_scalar(w1h_sb[:, c, :], w1h_sb[:, c, :], ghcol[:, c:c + 1],
                                 None, ALU.mult))
            v.wait('vs')
    w1h_mark = ctx.count('vs')

    for grp in range(NQ):
        for m in range(NHH):
            sl = m % 2

            @ctx.sec('tensor')
            def _(te, grp=grp, m=m, sl=sl):
                te.wait('vs', w1h_mark)
                te.wait('ss', rms_mark)
                te.wait('gs')
                te.wait('ss')
                if m < 2:
                    te.wait('vs')
                for c in range(DK):
                    te.op(te.matmul(ps_h[:, sl, :], w1h_sb[:, c, m * 128:(m + 1) * 128],
                                    rms16[:, c, grp * 512:(grp + 1) * 512],
                                    start=(c == 0), stop=False))
                te.op(te.matmul(ps_h[:, sl, :], b1h_sb[:, m * 128:(m + 1) * 128],
                                onesh16[:], start=False, stop=True))

            @ctx.sec('scalar')
            def _(s, grp=grp, m=m, sl=sl):
                s.wait('ts')
                s.op(s.activation(hh16[:, m, :], ps_h[:, sl, :], AF.Gelu))
        hh_mark = ctx.count('ss')

        for kc in range(NHH):
            ks = kc % 2

            @ctx.sec('gpsimd')
            def _(g, grp=grp, kc=kc, ks=ks):
                g.wait('ts')
                g.dma(w2hb[:, ks, :], W['w2h'][kc * 128:(kc + 1) * 128, :],
                      sem='w2%d' % ks)

            @ctx.sec('tensor')
            def _(te, grp=grp, kc=kc, ks=ks, hh_mark=hh_mark):
                te.wait('ss', hh_mark)
                te.wait('w2%d' % ks)
                if kc == 0:
                    te.wait('vs')
                for sub in range(4):
                    for half in range(2):
                        te.op(te.matmul(ps_h[:, sub * 2 + half, :],
                                        hh16[:, kc, sub * 128:(sub + 1) * 128],
                                        w2hb[:, ks, half * 512:(half + 1) * 512],
                                        start=(kc == 0), stop=False))
                if kc == NHH - 1:
                    for sub in range(4):
                        for half in range(2):
                            te.op(te.matmul(ps_h[:, sub * 2 + half, :],
                                            onesh16[:, 0:128],
                                            b2h_sb[:, half * 512:(half + 1) * 512],
                                            start=False, stop=True))
        w2_mark = ctx.count('ts')

        for sub in range(4):
            t = grp * 4 + sub
            sl = sub % 2

            @ctx.sec('sync')
            def _(s, t=t, sl=sl):
                s.wait('vs')
                s.wait('lo0', lo_marks[0])
                s.wait('lo1', lo_marks[1])
                s.dma(xlb[:, sl, 0, :], W['x_in'][t * 128:(t + 1) * 128, :],
                      sem='xf%d' % sl)
                s.dma(xlb[:, sl, 1, :], W['lightout'][t * 128:(t + 1) * 128, :],
                      sem='xf%d' % sl)

            @ctx.sec('vector')
            def _(v, t=t, sub=sub, sl=sl, w2_mark=w2_mark):
                v.wait('ts', w2_mark)
                v.wait('xf%d' % sl)
                v.wait('gw', gw_mark)
                v.wait('xs%d' % sl)
                v.op(v.tensor_add(xlb[:, sl, 0, :], xlb[:, sl, 0, :], xlb[:, sl, 1, :]))
                v.wait('vs')
                v.op(v.scalar_tensor_tensor(fin[:, sl, 0:512], ps_h[:, sub * 2, :],
                                            w_own[:, t:t + 1], xlb[:, sl, 0, 0:512],
                                            ALU.mult, ALU.add))
                v.wait('vs')
                v.op(v.scalar_tensor_tensor(fin[:, sl, 512:1024],
                                            ps_h[:, sub * 2 + 1, :],
                                            w_own[:, t:t + 1], xlb[:, sl, 0, 512:1024],
                                            ALU.mult, ALU.add))

            @ctx.sec('scalar')
            def _(s, t=t, sl=sl):
                s.wait('vs')
                s.dma(W['x_out'][t * 128:(t + 1) * 128, :], fin[:, sl, :],
                      sem='xs%d' % sl)
    phH.close()
    return (ctx.count('xs0'), ctx.count('xs1'))



def emit_ssm(ctx, H, W, prefix="sm", seq_glob=None):
    NT, T_LOC = H['NT'], H['T_LOC']
    ident16 = H['ident16']
    CH = T_LOC // 2
    SEQ = seq_glob
    SEQ2 = 2 * SEQ
    ST = 128
    NST = SEQ // ST

    lnx_fm = ctx.sbuf(prefix + "lnx_fm", [128, DK, T_LOC], BF16)
    fm_mark = emit_ln_to_fm(ctx, W, H, lnx_fm, prefix)

    @ctx.sec('gpsimd')
    def _(g):
        g.wait('ss', fm_mark)
        for c in range(DK):
            g.dma(W['lnag_in'][c * 128:(c + 1) * 128, :], lnx_fm[:, c, :], sem='li')
        g.wait('li')
        g.op(g.collective_compute(
            "AllGather", ALU.bypass, replica_groups=[list(range(NCORE))],
            ins=[W['lnag_in']], outs=[W['lnag_out']]), sem='cc')
    ag_mark = ctx.count('cc')

    ph2 = contextlib.ExitStack()
    wiu = ctx.sbuf(prefix + "wiu", [128, DK, 256], BF16, es=ph2)
    wig = ctx.sbuf(prefix + "wig", [128, DK, 256], BF16, es=ph2)
    lnstr = ctx.sbuf(prefix + "lnstr", [128, 2, DK, 512], BF16, es=ph2)
    ugev = ctx.sbuf(prefix + "ugev", [128, 2, 4, 512], F32, es=ph2)
    ps_u = ctx.psum(prefix + "ps_u", [128, 4, 512], F32, es=ph2)

    @ctx.sec('gpsimd')
    def _(g):
        g.dma(wiu[:], W['w_in_uT'].rearrange("(c p) m -> p c m", p=128))
        g.dma(wig[:], W['w_in_gT'].rearrange("(c p) m -> p c m", p=128))
    wload = ctx.count('dgp')

    NJ = SEQ2 // 512
    for j in range(NJ):
        bsel = j % 2

        @ctx.sec('sync')
        def _(s, j=j, bsel=bsel):
            s.wait('cc', ag_mark)
            if j >= 2:
                s.wait('ts')
            gidx = j * 512
            b = gidx // SEQ
            pos = gidx % SEQ
            r = pos // CH if b == 0 else 7 - (pos // CH)
            l = (pos % CH) + (0 if b == 0 else CH)
            s.dma(lnstr[:, bsel, :, :],
                  W['lnag_out'][r * D:(r + 1) * D, l:l + 512]
                  .rearrange("(c p) t -> p c t", p=128), sem='ln%d' % bsel)

        @ctx.sec('tensor')
        def _(te, j=j, bsel=bsel):
            te.wait('ln%d' % bsel)
            te.wait('dgp', wload)
            te.wait('ss')
            for m in range(2):
                for c in range(DK):
                    te.op(te.matmul(ps_u[:, m, :], wiu[:, c, m * 128:(m + 1) * 128],
                                    lnstr[:, bsel, c, :], start=(c == 0),
                                    stop=(c == DK - 1)))
                for c in range(DK):
                    te.op(te.matmul(ps_u[:, 2 + m, :], wig[:, c, m * 128:(m + 1) * 128],
                                    lnstr[:, bsel, c, :], start=(c == 0),
                                    stop=(c == DK - 1)))

        @ctx.sec('scalar')
        def _(s, j=j, bsel=bsel):
            s.wait('ts')
            s.wait('ug%d' % bsel)
            for m in range(2):
                s.op(s.copy(ugev[:, bsel, m, :], ps_u[:, m, :]))
                s.wait('ss')
                s.dma(W['u_dram'][m * 128:(m + 1) * 128, j * 512:(j + 1) * 512],
                      ugev[:, bsel, m, :], sem='ug%d' % bsel)
                s.op(s.copy(ugev[:, bsel, 2 + m, :], ps_u[:, 2 + m, :]))
                s.wait('ss')
                s.dma(W['gate_dram'][m * 128:(m + 1) * 128, j * 512:(j + 1) * 512],
                      ugev[:, bsel, 2 + m, :], sem='ug%d' % bsel)
    ph2_mark = (ctx.count('ug0'), ctx.count('ug1'))
    ph2.close()

    ph3 = contextlib.ExitStack()
    ccol = ctx.sbuf(prefix + "ccol", [128, 2, 4], F32, es=ph3)
    wssmT_sb = ctx.sbuf(prefix + "wssmT", [128, 2, 96], BF16, es=ph3)
    ub = ctx.sbuf(prefix + "ub", [128, 2, 2048 + 3], F32, es=ph3)
    uc = ctx.sbuf(prefix + "uc", [128, 2, 2048], F32, es=ph3)
    us = ctx.sbuf(prefix + "us", [128, 2, 2048], F32, es=ph3)
    us16 = ctx.sbuf(prefix + "us16", [128, 2, 2048], BF16, es=ph3)
    ppev = ctx.sbuf(prefix + "ppev", [96, 2, 512], F32, es=ph3)
    ps_p = ctx.psum(prefix + "ps_p", [96, 2, 512], F32, es=ph3)

    @ctx.sec('gpsimd')
    def _(g):
        g.wait('ts')
        g.wait('ss')
        g.dma(ccol[:], W['conv_c'].rearrange("(c p) t -> p c t", p=128))
        g.dma(wssmT_sb[:], W['wssmT'].rearrange("(c p) m -> p c m", p=128))
    p3load = ctx.count('dgp')

    for b in range(2):
        for tt in range(SEQ // 2048):
            g0 = b * SEQ + tt * 2048

            @ctx.sec('sync')
            def _(s, b=b, tt=tt, g0=g0):
                s.wait('ug0', ph2_mark[0])
                s.wait('ug1', ph2_mark[1])
                s.wait('vs')
                if tt == 0:
                    s.dma(ub[:, :, 3:2051], W['u_dram'][:, g0:g0 + 2048]
                          .rearrange("(c p) t -> p c t", p=128), sem='ul')
                else:
                    s.dma(ub[:, :, 0:2051], W['u_dram'][:, g0 - 3:g0 + 2048]
                          .rearrange("(c p) t -> p c t", p=128), sem='ul')

            @ctx.sec('gpsimd')
            def _(g, b=b, tt=tt):
                if tt == 0:
                    g.wait('vs')
                    g.op(g.memset(ub[:, :, 0:3], 0.0))

            @ctx.sec('vector')
            def _(v, b=b, tt=tt):
                v.wait('ul')
                v.wait('gs')
                v.wait('dgp', p3load)
                for c2 in range(2):
                    v.op(v.tensor_scalar(uc[:, c2, :], ub[:, c2, 0:2048],
                                         ccol[:, c2, 0:1], None, ALU.mult))
                    v.wait('vs')
                    for jj in range(1, 4):
                        v.op(v.scalar_tensor_tensor(uc[:, c2, :],
                                                    ub[:, c2, jj:jj + 2048],
                                                    ccol[:, c2, jj:jj + 1],
                                                    uc[:, c2, :], ALU.mult, ALU.add))
                        v.wait('vs')
            @ctx.sec('scalar')
            def _(s, b=b, tt=tt):
                s.wait('vs')
                s.wait('us0')
                s.op(s.activation(us[:], uc[:], AF.Silu))

            @ctx.sec('vector')
            def _(v, b=b, tt=tt):
                v.wait('ss')
                v.wait('us%d' % 0)
                v.op(v.tensor_copy(us16[:], us[:]))
                v.wait('vs')

            @ctx.sec('scalar')
            def _(s, b=b, tt=tt, g0=g0):
                s.wait('vs')
                s.dma(W['usil_dram'][:, g0:g0 + 2048]
                      .rearrange("(c p) t -> p c t", p=128), us[:], sem='us0')

            for n in range(4):
                @ctx.sec('tensor')
                def _(te, b=b, tt=tt, n=n):
                    te.wait('vs')
                    te.wait('dgp', p3load)
                    te.wait('ss')
                    for c in range(2):
                        te.op(te.matmul(ps_p[:, n % 2, :], wssmT_sb[:, c, :],
                                        us16[:, c, n * 512:(n + 1) * 512],
                                        start=(c == 0), stop=(c == 1)))

                @ctx.sec('scalar')
                def _(s, b=b, tt=tt, n=n, g0=g0):
                    s.wait('ts')
                    s.wait('pp%d' % (n % 2))
                    s.op(s.copy(ppev[:, n % 2, :], ps_p[:, n % 2, :]))
                    s.wait('ss')
                    s.dma(W['pp_dram'][:, g0 + n * 512:g0 + (n + 1) * 512],
                          ppev[:, n % 2, :], sem='pp%d' % (n % 2))
    p3_marks = (ctx.count('pp0'), ctx.count('pp1'), ctx.count('us0'))
    ph3.close()

    @ctx.sec('gpsimd')
    def _(g):
        g.wait('pp0', p3_marks[0])
        g.wait('pp1', p3_marks[1])
        g.op(g.collective_compute(
            "AllReduce", ALU.add, replica_groups=[list(range(NCORE))],
            ins=[W['pp_dram']], outs=[W['params']]), sem='cc')
    ar_mark = ctx.count('cc')

    ph5 = contextlib.ExitStack()
    wdtT_sb = ctx.sbuf(prefix + "wdtT", [64, 256], BF16, es=ph5)
    bdt_c = ctx.sbuf(prefix + "bdt", [128, 2], F32, es=ph5)
    A_sc = ctx.sbuf(prefix + "A_sc", [128, 2, 16], F32, es=ph5)
    D_cc = ctx.sbuf(prefix + "D_cc", [128, 2], F32, es=ph5)
    onesf1 = ctx.sbuf(prefix + "onesf1", [1, 128], F32, es=ph5)
    prm = ctx.sbuf(prefix + "prm", [96, ST], F32, es=ph5)
    prm16 = ctx.sbuf(prefix + "prm16", [96, ST], BF16, es=ph5)
    bcrow = ctx.sbuf(prefix + "bcrow", [1, 2, 16 * ST], F32, es=ph5)
    dtf = ctx.sbuf(prefix + "dtf", [128, 2, ST], F32, es=ph5)
    zsb = ctx.sbuf(prefix + "zsb", [128, 2, ST], F32, es=ph5)
    z16 = ctx.sbuf(prefix + "z16", [128, 2, ST], BF16, es=ph5)
    usl = ctx.sbuf(prefix + "usl", [128, 2, ST], F32, es=ph5)
    gal = ctx.sbuf(prefix + "gal", [128, 2, ST], F32, es=ph5)
    aarr = ctx.sbuf(prefix + "aarr", [128, 16, ST], F32, es=ph5)
    xarr = ctx.sbuf(prefix + "xarr", [128, 16, ST], F32, es=ph5)
    harr = ctx.sbuf(prefix + "harr", [128, 16, ST], F32, es=ph5)
    garr = ctx.sbuf(prefix + "garr", [128, 16, ST], F32, es=ph5)
    hcar = ctx.sbuf(prefix + "hcar", [128, 2, 2, 16], F32, es=ph5)
    a0sv = ctx.sbuf(prefix + "a0sv", [128, 16], F32, es=ph5)
    yt = ctx.sbuf(prefix + "yt", [128, 2, ST], F32, es=ph5)
    ps_dt = ctx.psum(prefix + "ps_dt", [128, 2, ST], F32, es=ph5)
    ps_bc = ctx.psum(prefix + "ps_bc", [128, 16 * ST], F32, es=ph5)

    @ctx.sec('gpsimd')
    def _(g):
        g.wait('ts')
        g.wait('ss')
        g.wait('vs')
        g.dma(wdtT_sb[:], W['wdtT'])
        g.dma(bdt_c[:].unsqueeze(2), W['b_dt_c'].rearrange("(c p) one -> p c one", p=128),
              allow_slow_non_contiguous=True)
        g.dma(A_sc[:], W['logA_c'].rearrange("(c p) n -> p c n", p=128))
        g.dma(D_cc[:].unsqueeze(2), W['D_c'].rearrange("(c p) one -> p c one", p=128),
              allow_slow_non_contiguous=True)
        g.op(g.memset(onesf1[:], 1.0))
        g.op(g.memset(hcar[:], 0.0))
    p5load = ctx.count('dgp')
    p5g = ctx.count('gs')

    @ctx.sec('scalar')
    def _(s):
        s.wait('dgp', p5load)
        s.op(s.activation(A_sc[:], A_sc[:], AF.Exp))

    @ctx.sec('vector')
    def _(v):
        v.wait('ss')
        v.op(v.tensor_scalar(A_sc[:], A_sc[:], -1.0, None, ALU.mult))
    a_mark = ctx.count('vs')

    for b in range(2):
        for st in range(NST):
            g0 = b * SEQ + st * ST

            @ctx.sec('sync')
            def _(s, b=b, st=st, g0=g0):
                s.wait('cc', ar_mark)
                s.wait('us0', p3_marks[2])
                s.wait('vs')
                s.wait('ts')
                s.dma(prm[:], W['params'][:, g0:g0 + ST], sem='p5a')
                s.dma(usl[:], W['usil_dram'][:, g0:g0 + ST]
                      .rearrange("(c p) t -> p c t", p=128), sem='p5a')
                s.dma(gal[:], W['gate_dram'][:, g0:g0 + ST]
                      .rearrange("(c p) t -> p c t", p=128), sem='p5a')

            @ctx.sec('scalar')
            def _(s, b=b, st=st):
                s.wait('p5a')
                s.wait('ts')
                s.dma(bcrow[:, 0, :], prm[64:80, :], sem='p5b')
                s.dma(bcrow[:, 1, :], prm[80:96, :], sem='p5b')

            @ctx.sec('vector')
            def _(v, b=b, st=st):
                v.wait('p5a')
                v.op(v.tensor_copy(prm16[0:64, :], prm[0:64, :]))
            prm_mark = ctx.count('vs')

            @ctx.sec('tensor')
            def _(te, b=b, st=st, prm_mark=prm_mark):
                te.wait('vs', prm_mark)
                te.wait('dgp', p5load)
                te.wait('vs')
                for c2 in range(2):
                    te.op(te.matmul(ps_dt[:, c2, :], wdtT_sb[:, c2 * 128:(c2 + 1) * 128],
                                    prm16[0:64, :], start=True, stop=True))
            bc_mark = ctx.count('ts')

            @ctx.sec('vector')
            def _(v, b=b, st=st, bc_mark=bc_mark):
                v.wait('ts', bc_mark)
                v.op(v.tensor_scalar(dtf[:], ps_dt[:].rearrange("p c t -> p c t"),
                                     bdt_c[:, 0:1], None, ALU.add) if False else
                     v.tensor_scalar(dtf[:, 0, :], ps_dt[:, 0, :], bdt_c[:, 0:1],
                                     None, ALU.add))
                v.wait('vs')
                v.op(v.tensor_scalar(dtf[:, 1, :], ps_dt[:, 1, :], bdt_c[:, 1:2],
                                     None, ALU.add))
                v.wait('vs')
                v.op(v.tensor_scalar(zsb[:], dtf[:], 0.0, None, ALU.max))

            @ctx.sec('scalar')
            def _(s, b=b, st=st):
                s.wait('vs')
                s.op(s.activation(yt[:], dtf[:], AF.Abs))
                s.wait('ss')
                s.op(s.activation(yt[:], yt[:], AF.Exp, scale=-1.0))
                s.wait('ss')
                s.op(s.activation(yt[:], yt[:], AF.Ln, bias=1.0))

            @ctx.sec('vector')
            def _(v, b=b, st=st):
                v.wait('ss')
                v.op(v.tensor_add(dtf[:], zsb[:], yt[:]))
                v.wait('vs')

            for c2 in range(2):
                @ctx.sec('tensor')
                def _(te, c2=c2):
                    te.wait('gs', p5g)
                    te.wait('p5b')
                    te.wait('vs')
                    for off in range(0, 16 * ST, 512):
                        te.op(te.matmul(ps_bc[:, off:off + 512], onesf1[:],
                                        bcrow[:, 0, off:off + 512],
                                        start=True, stop=True))

                @ctx.sec('vector')
                def _(v, b=b, st=st, c2=c2):
                    v.wait('vs')
                    v.wait('vs', a_mark)
                    v.op(v.tensor_tensor(
                        aarr[:], dtf[:, c2, :].unsqueeze(1).broadcast_to([128, 16, ST]),
                        A_sc[:, c2, :].unsqueeze(2).broadcast_to([128, 16, ST]),
                        ALU.mult))
                    v.wait('vs')

                @ctx.sec('scalar')
                def _(s, b=b, st=st, c2=c2):
                    s.wait('vs')
                    s.op(s.activation(aarr[:], aarr[:], AF.Exp))
                aexp_mark = ctx.count('ss')

                @ctx.sec('vector')
                def _(v, b=b, st=st, c2=c2, aexp_mark=aexp_mark, bc_mark=bc_mark):
                    v.wait('ss', aexp_mark)
                    v.wait('ts')
                    v.op(v.tensor_tensor(yt[:, c2, :], dtf[:, c2, :], usl[:, c2, :],
                                         ALU.mult))
                    v.wait('vs')
                    v.op(v.tensor_tensor(
                        xarr[:], yt[:, c2, :].unsqueeze(1).broadcast_to([128, 16, ST]),
                        ps_bc[:].rearrange("p (n t) -> p n t", n=16), ALU.mult))
                    v.wait('vs')
                    v.op(v.tensor_copy(a0sv[:], aarr[:, :, 0]))
                    v.wait('vs')
                    v.op(v.memset(aarr[:, :, 0:1], 0.0))
                    v.wait('vs')
                    v.op(v.scalar_tensor_tensor(xarr[:, :, 0:1],
                                                hcar[:, b, c2, :].unsqueeze(2), 1.0,
                                                xarr[:, :, 0:1], ALU.mult, ALU.add)
                         if False else
                         v.tensor_tensor(a0sv[:], a0sv[:], hcar[:, b, c2, :], ALU.mult))
                    v.wait('vs')
                    v.op(v.tensor_add(xarr[:, :, 0], xarr[:, :, 0], a0sv[:]))
                    v.wait('vs')
                    v.op(v.tensor_tensor_scan(
                        harr[:].rearrange("p n t -> p (n t)"),
                        aarr[:].rearrange("p n t -> p (n t)"),
                        xarr[:].rearrange("p n t -> p (n t)"), 0.0,
                        ALU.mult, ALU.add))
                    v.wait('vs')
                    v.op(v.tensor_copy(hcar[:, b, c2, :], harr[:, :, ST - 1]))
                g_mark1 = ctx.count('vs')

                @ctx.sec('tensor')
                def _(te, c2=c2, g_mark1=g_mark1):
                    te.wait('vs', g_mark1)
                    for off in range(0, 16 * ST, 512):
                        te.op(te.matmul(ps_bc[:, off:off + 512], onesf1[:],
                                        bcrow[:, 1, off:off + 512],
                                        start=True, stop=True))

                @ctx.sec('vector')
                def _(v, b=b, st=st, c2=c2):
                    v.wait('ts')
                    v.op(v.tensor_tensor(garr[:], harr[:],
                                         ps_bc[:].rearrange("p (n t) -> p n t", n=16),
                                         ALU.mult))
                    v.wait('vs')
                    v.op(v.raw.tensor_reduce(yt[:, c2, :],
                                             garr[:].transpose([0, 2, 1]), AX.X,
                                             ALU.add))
                    v.wait('vs')
                    v.op(v.scalar_tensor_tensor(yt[:, c2, :], usl[:, c2, :],
                                                D_cc[:, c2:c2 + 1], yt[:, c2, :],
                                                ALU.mult, ALU.add))
                    v.wait('vs')

            @ctx.sec('scalar')
            def _(s, b=b, st=st):
                s.wait('vs')
                s.op(s.activation(zsb[:], gal[:], AF.Silu))

            @ctx.sec('vector')
            def _(v, b=b, st=st):
                v.wait('ss')
                v.wait('zo0')
                v.op(v.tensor_tensor(z16[:], yt[:], zsb[:], ALU.mult))
            z_mark = ctx.count('vs')

            @ctx.sec('scalar')
            def _(s, b=b, st=st, g0=g0, z_mark=z_mark):
                s.wait('vs', z_mark)
                pos = st * ST
                r = pos // CH if b == 0 else 7 - (pos // CH)
                l = (pos % CH) + (0 if b == 0 else CH)
                s.dma(W['zin'][r * 256:(r + 1) * 256, l:l + ST]
                      .rearrange("(c p) t -> p c t", p=128), z16[:], sem='zo0')
    z_done = ctx.count('zo0')
    ph5.close()

    @ctx.sec('gpsimd')
    def _(g):
        g.wait('zo0', z_done)
        from concourse.aot_env import allow_ici_all_to_all
        with allow_ici_all_to_all():
            g.op(g.collective_compute(
                "AllToAll", ALU.bypass, replica_groups=[list(range(NCORE))],
                ins=[W['zin']], outs=[W['zout']]), sem='cc')
    a2a_mark = ctx.count('cc')

    ph7 = contextlib.ExitStack()
    woT = ctx.sbuf(prefix + "woT", [128, 16, D], BF16, es=ph7)
    zst = ctx.sbuf(prefix + "zst", [128, 2, 16, 128], BF16, es=ph7)
    xb = ctx.sbuf(prefix + "xb", [128, 2, D], F32, es=ph7)
    ps_o = ctx.psum(prefix + "ps_o", [128, 2, 512], F32, es=ph7)

    @ctx.sec('gpsimd')
    def _(g):
        g.wait('ts')
        g.wait('vs')
        g.wait('ss')
        g.dma(woT[:], W['woutT'].rearrange("(c p) m -> p c m", p=128))
    p7load = ctx.count('dgp')

    for t in range(NT):
        sl = t % 2

        @ctx.sec('sync')
        def _(s, t=t, sl=sl):
            s.wait('cc', a2a_mark)
            s.wait('ts')
            s.wait('xo%d' % sl)
            s.dma(zst[:, sl, :, :], W['zout'][:, t * 128:(t + 1) * 128]
                  .rearrange("(c p) t -> p c t", p=128), sem='zl%d' % sl)
            s.dma(xb[:, sl, :], W['x_in'][t * 128:(t + 1) * 128, :], sem='zl%d' % sl)

        @ctx.sec('tensor')
        def _(te, t=t, sl=sl):
            te.wait('zl%d' % sl)
            te.wait('dgp', p7load)
            te.wait('vs')
            for half in range(2):
                for kc in range(16):
                    te.op(te.matmul(ps_o[:, half, :], zst[:, sl, kc, :],
                                    woT[:, kc, half * 512:(half + 1) * 512],
                                    start=(kc == 0), stop=(kc == 15)))

        @ctx.sec('vector')
        def _(v, t=t, sl=sl):
            v.wait('ts')
            v.op(v.tensor_add(xb[:, sl, 0:512], xb[:, sl, 0:512], ps_o[:, 0, :]))
            v.wait('vs')
            v.op(v.tensor_add(xb[:, sl, 512:1024], xb[:, sl, 512:1024], ps_o[:, 1, :]))

        @ctx.sec('scalar')
        def _(s, t=t, sl=sl):
            s.wait('vs')
            s.dma(W['x_out'][t * 128:(t + 1) * 128, :], xb[:, sl, :], sem='xo%d' % sl)
    ph7.close()
    return (ctx.count('xo0'), ctx.count('xo1'))
import ml_dtypes

T_LOC = 2048
CH = T_LOC // 2
SEQ = NCORE * CH
NT = T_LOC // 128
NTB = NT // 2
NTG = SEQ // 128

_prog_cache = {}


def _ctx(nc, es):
    ctx = Ctx(nc, es)
    return ctx


def build_r1():
    nc = bass.Bass("TRN2", target_bir_lowering=False)
    W = {}
    W['x_in'] = nc.dram_tensor("x_in", [T_LOC, D], F32, kind="ExternalInput").ap()
    for n in ("Wq", "Wk", "Wv"):
        W[n] = nc.dram_tensor(n, [D, D], F32, kind="ExternalInput").ap()
    W['theta'] = nc.dram_tensor("theta", [D, 1], F32, kind="ExternalInput").ap()
    W['q_out'] = nc.dram_tensor("q_out", [D, T_LOC], BF16, kind="ExternalOutput").ap()
    W['k_out'] = nc.dram_tensor("k_out", [D, T_LOC], BF16, kind="ExternalOutput").ap()
    W['v_out'] = nc.dram_tensor("v_out", [T_LOC, D], BF16, kind="ExternalOutput").ap()
    with contextlib.ExitStack() as es:
        ctx = _ctx(nc, es)
        ident16 = emit_identity(ctx, "ident16", BF16, 128)
        H = dict(ident16=ident16, NT=NT, T_LOC=T_LOC)
        emit_retnet_proj(ctx, H, W)

        @ctx.sec('gpsimd')
        def _(g):
            g.wait('pe0')
            g.wait('pe1')
    return nc


def build_r2():
    nc = bass.Bass("TRN2", target_bir_lowering=False)
    WIN = RET_WIN + CH
    W = {}
    W['x_in'] = nc.dram_tensor("x_in", [T_LOC, D], F32, kind="ExternalInput").ap()
    W['q_fm'] = nc.dram_tensor("q_fm", [D, T_LOC], BF16, kind="ExternalInput").ap()
    W['k_win'] = nc.dram_tensor("k_win", [2, D, WIN], BF16, kind="ExternalInput").ap()
    W['v_win'] = nc.dram_tensor("v_win", [2, WIN, D], BF16, kind="ExternalInput").ap()
    W['x_out'] = nc.dram_tensor("x_out", [T_LOC, D], F32, kind="ExternalOutput").ap()
    with contextlib.ExitStack() as es:
        ctx = _ctx(nc, es)
        ident16 = emit_identity(ctx, "ident16", BF16, 128)
        identf = emit_identity(ctx, "identf", F32, 128)
        H = dict(ident16=ident16, identf=identf, NT=NT, T_LOC=T_LOC)
        done = emit_retnet_attn(ctx, H, W)

        @ctx.sec('gpsimd')
        def _(g):
            g.wait('xso0', done[0])
            g.wait('xso1', done[1])
    return nc


def build_moe():
    nc = bass.Bass("TRN2", target_bir_lowering=False)
    T_GLOB = NCORE * T_LOC
    W = {}
    W['x_in'] = nc.dram_tensor("x_in", [T_LOC, D], F32, kind="ExternalInput").ap()
    W['gate_W'] = nc.dram_tensor("gate_W", [D, 8], F32, kind="ExternalInput").ap()
    W['W1_e'] = nc.dram_tensor("W1_e", [D, 2048], F32, kind="ExternalInput").ap()
    W['b1_e'] = nc.dram_tensor("b1_e", [1, 2048], F32, kind="ExternalInput").ap()
    W['W2_e'] = nc.dram_tensor("W2_e", [2048, D], F32, kind="ExternalInput").ap()
    W['b2_e'] = nc.dram_tensor("b2_e", [1, D], F32, kind="ExternalInput").ap()
    W['eselc'] = nc.dram_tensor("eselc", [8, 1], F32, kind="ExternalInput").ap()
    W['x_out'] = nc.dram_tensor("x_out", [T_LOC, D], F32, kind="ExternalOutput").ap()
    W['lnag_in'] = nc.dram_tensor("lnag_in", [D, T_LOC], BF16).ap()
    W['lnag_out'] = nc.dram_tensor("lnag_out", [NCORE * D, T_LOC], BF16).ap()
    W['cw_ag_in'] = nc.dram_tensor("cw_ag_in", [8, T_LOC], F32).ap()
    W['cw_ag_out'] = nc.dram_tensor("cw_ag_out", [NCORE * 8, T_LOC], F32).ap()
    W['partials'] = nc.dram_tensor("partials", [T_GLOB, D], F32).ap()
    W['rs_out'] = nc.dram_tensor("rs_out", [T_LOC, D], F32).ap()
    with contextlib.ExitStack() as es:
        ctx = _ctx(nc, es)
        ident16 = emit_identity(ctx, "ident16", BF16, 128)
        identf = emit_identity(ctx, "identf", F32, 128)
        ones1024 = emit_ones_row(ctx, "ones1024", 1024, BF16)
        H = dict(ident16=ident16, identf=identf, ones1024=ones1024, NT=NT,
                 T_LOC=T_LOC, T_GLOB=T_GLOB)
        done = emit_moe(ctx, H, W)

        @ctx.sec('gpsimd')
        def _(g):
            g.wait('xfs0', done[0])
            g.wait('xfs1', done[1])
            g.wait('dsy')
    return nc


def build_b1():
    nc = bass.Bass("TRN2", target_bir_lowering=False)
    W = {}
    W['x_in'] = nc.dram_tensor("x_in", [T_LOC, D], F32, kind="ExternalInput").ap()
    W['Wq'] = nc.dram_tensor("Wq", [D, D], F32, kind="ExternalInput").ap()
    W['Wkv'] = nc.dram_tensor("Wkv", [D, 128], F32, kind="ExternalInput").ap()
    W['Wk'] = nc.dram_tensor("Wk", [128, 512], F32, kind="ExternalInput").ap()
    W['Wv'] = nc.dram_tensor("Wv", [128, 512], F32, kind="ExternalInput").ap()
    W['qg_row'] = nc.dram_tensor("qg_row", [1, D], F32, kind="ExternalInput").ap()
    W['kg_row'] = nc.dram_tensor("kg_row", [1, 512], F32, kind="ExternalInput").ap()
    W['q_pre'] = nc.dram_tensor("q_pre", [16, 34, T_LOC], BF16, kind="ExternalOutput").ap()
    W['k_pre'] = nc.dram_tensor("k_pre", [16, 34, T_LOC], BF16, kind="ExternalOutput").ap()
    W['v_pre'] = nc.dram_tensor("v_pre", [T_LOC, 512], BF16, kind="ExternalOutput").ap()
    with contextlib.ExitStack() as es:
        ctx = _ctx(nc, es)
        ident16 = emit_identity(ctx, "ident16", BF16, 128)
        H = dict(ident16=ident16, NT=NT, T_LOC=T_LOC)
        emit_attnb_proj(ctx, H, W)

        @ctx.sec('gpsimd')
        def _(g):
            for n in ('qx0', 'qx1', 'vx0', 'vx1'):
                g.wait(n)
    return nc


def build_b2():
    nc = bass.Bass("TRN2", target_bir_lowering=False)
    WIN = SEQ + CH
    W = {}
    W['x_in'] = nc.dram_tensor("x_in", [T_LOC, D], F32, kind="ExternalInput").ap()
    W['q_pre'] = nc.dram_tensor("q_pre", [16, 34, T_LOC], BF16, kind="ExternalInput").ap()
    W['k_win'] = nc.dram_tensor("k_win", [2, 16, 34, WIN], BF16, kind="ExternalInput").ap()
    W['v_win'] = nc.dram_tensor("v_win", [2, WIN, 16, 33], BF16, kind="ExternalInput").ap()
    W['wo_pk'] = nc.dram_tensor("wo_pk", [D, D], F32, kind="ExternalInput").ap()
    W['x_out'] = nc.dram_tensor("x_out", [T_LOC, D], F32, kind="ExternalOutput").ap()
    with contextlib.ExitStack() as es:
        ctx = _ctx(nc, es)
        ident16 = emit_identity(ctx, "ident16", BF16, 128)
        ones32 = emit_ones_row(ctx, "ones32", 32, F32)
        H = dict(ident16=ident16, ones32=ones32, NT=NT, T_LOC=T_LOC)
        done = emit_attnb_attn(ctx, H, W, win_seq=SEQ)

        @ctx.sec('gpsimd')
        def _(g):
            g.wait('xo0', done[0])
            g.wait('xo1', done[1])
    return nc


def build_a1():
    nc = bass.Bass("TRN2", target_bir_lowering=False)
    W = {}
    W['x_in'] = nc.dram_tensor("x_in", [T_LOC, D], F32, kind="ExternalInput").ap()
    W['Wqkv'] = nc.dram_tensor("Wqkv", [64, 192], F32, kind="ExternalInput").ap()
    W['bqkv'] = nc.dram_tensor("bqkv", [1, 192], F32, kind="ExternalInput").ap()
    W['q_pre'] = nc.dram_tensor("q_pre", [16, 66, T_LOC], BF16, kind="ExternalOutput").ap()
    W['k_pre'] = nc.dram_tensor("k_pre", [16, 66, T_LOC], BF16, kind="ExternalOutput").ap()
    W['v_pre'] = nc.dram_tensor("v_pre", [T_LOC, D], BF16, kind="ExternalOutput").ap()
    W['tilesums'] = nc.dram_tensor("tilesums", [NT, D], F32, kind="ExternalOutput").ap()
    with contextlib.ExitStack() as es:
        ctx = _ctx(nc, es)
        ident16 = emit_identity(ctx, "ident16", BF16, 128)
        H = dict(ident16=ident16, NT=NT, T_LOC=T_LOC)
        emit_attna_proj(ctx, H, W)

        @ctx.sec('gpsimd')
        def _(g):
            for n in ('qx0', 'qx1', 'vx0', 'vx1', 'tx0', 'tx1'):
                g.wait(n)
    return nc


def build_a2():
    nc = bass.Bass("TRN2", target_bir_lowering=False)
    WIN = SEQ + CH
    W = {}
    W['x_in'] = nc.dram_tensor("x_in", [T_LOC, D], F32, kind="ExternalInput").ap()
    W['q_pre'] = nc.dram_tensor("q_pre", [16, 66, T_LOC], BF16, kind="ExternalInput").ap()
    W['k_win'] = nc.dram_tensor("k_win", [2, 16, 66, WIN], BF16, kind="ExternalInput").ap()
    W['v_win'] = nc.dram_tensor("v_win", [2, WIN, 16, 65], BF16, kind="ExternalInput").ap()
    W['v_aug'] = nc.dram_tensor("v_aug", [T_LOC, 1152], BF16, kind="ExternalInput").ap()
    W['tsaug'] = nc.dram_tensor("tsaug", [2, NTG, 1152], F32, kind="ExternalInput").ap()
    W['msel'] = nc.dram_tensor("msel", [2, NTG, NTB], F32, kind="ExternalInput").ap()
    W['wr'] = nc.dram_tensor("wr", [D, D], F32, kind="ExternalInput").ap()
    W['br'] = nc.dram_tensor("br", [1, D], F32, kind="ExternalInput").ap()
    W['x_out'] = nc.dram_tensor("x_out", [T_LOC, D], F32, kind="ExternalOutput").ap()
    W['qaug_dram'] = nc.dram_tensor("qaug_dram", [16, 66, T_LOC], BF16).ap()
    with contextlib.ExitStack() as es:
        ctx = _ctx(nc, es)
        ident16 = emit_identity(ctx, "ident16", BF16, 128)
        identf = emit_identity(ctx, "identf", F32, 128)
        H = dict(ident16=ident16, identf=identf, NT=NT, T_LOC=T_LOC)
        done = emit_attna_attn(ctx, H, W, win_seq=SEQ, seq_glob=SEQ)

        @ctx.sec('gpsimd')
        def _(g):
            g.wait('xo0', done[0])
            g.wait('xo1', done[1])
    return nc


def build_lf():
    nc = bass.Bass("TRN2", target_bir_lowering=False)
    W = {}
    W['x_in'] = nc.dram_tensor("x_in", [T_LOC, D], F32, kind="ExternalInput").ap()
    W['rt_row'] = nc.dram_tensor("rt_row", [1, D], F32, kind="ExternalInput").ap()
    W['gl'] = nc.dram_tensor("gl", [D, 1], F32, kind="ExternalInput").ap()
    W['gh'] = nc.dram_tensor("gh", [D, 1], F32, kind="ExternalInput").ap()
    W['w1l'] = nc.dram_tensor("w1l", [D, 512], F32, kind="ExternalInput").ap()
    W['b1l'] = nc.dram_tensor("b1l", [1, 512], F32, kind="ExternalInput").ap()
    W['w2l'] = nc.dram_tensor("w2l", [512, D], F32, kind="ExternalInput").ap()
    W['b2l'] = nc.dram_tensor("b2l", [1, D], F32, kind="ExternalInput").ap()
    W['w1h'] = nc.dram_tensor("w1h", [D, 4096], F32, kind="ExternalInput").ap()
    W['b1h'] = nc.dram_tensor("b1h", [1, 4096], F32, kind="ExternalInput").ap()
    W['w2h'] = nc.dram_tensor("w2h", [4096, D], F32, kind="ExternalInput").ap()
    W['b2h'] = nc.dram_tensor("b2h", [1, D], F32, kind="ExternalInput").ap()
    W['widx'] = nc.dram_tensor("widx", [T_LOC, 1], mybir.dt.uint32, kind="ExternalInput").ap()
    W['x_out'] = nc.dram_tensor("x_out", [T_LOC, D], F32, kind="ExternalOutput").ap()
    W['s_ag_in'] = nc.dram_tensor("s_ag_in", [2 * NT, 64], F32).ap()
    W['s_ag_out'] = nc.dram_tensor("s_ag_out", [NCORE * 2 * NT, 64], F32).ap()
    W['w_dram'] = nc.dram_tensor("w_dram", [2, SEQ // 64, 64], F32).ap()
    W['lightout'] = nc.dram_tensor("lightout", [T_LOC, D], F32).ap()
    with contextlib.ExitStack() as es:
        ctx = _ctx(nc, es)
        ident16 = emit_identity(ctx, "ident16", BF16, 128)
        H = dict(ident16=ident16, NT=NT, T_LOC=T_LOC)
        done = emit_l1ffn(ctx, H, W, seq_glob=SEQ)

        @ctx.sec('gpsimd')
        def _(g):
            g.wait('xs0', done[0])
            g.wait('xs1', done[1])
    return nc


def build_sm():
    nc = bass.Bass("TRN2", target_bir_lowering=False)
    SEQ2 = 2 * SEQ
    W = {}
    W['x_in'] = nc.dram_tensor("x_in", [T_LOC, D], F32, kind="ExternalInput").ap()
    W['w_in_uT'] = nc.dram_tensor("w_in_uT", [D, 256], F32, kind="ExternalInput").ap()
    W['w_in_gT'] = nc.dram_tensor("w_in_gT", [D, 256], F32, kind="ExternalInput").ap()
    W['conv_c'] = nc.dram_tensor("conv_c", [256, 4], F32, kind="ExternalInput").ap()
    W['wssmT'] = nc.dram_tensor("wssmT", [256, 96], F32, kind="ExternalInput").ap()
    W['wdtT'] = nc.dram_tensor("wdtT", [64, 256], F32, kind="ExternalInput").ap()
    W['b_dt_c'] = nc.dram_tensor("b_dt_c", [256, 1], F32, kind="ExternalInput").ap()
    W['logA_c'] = nc.dram_tensor("logA_c", [256, 16], F32, kind="ExternalInput").ap()
    W['D_c'] = nc.dram_tensor("D_c", [256, 1], F32, kind="ExternalInput").ap()
    W['woutT'] = nc.dram_tensor("woutT", [2048, D], F32, kind="ExternalInput").ap()
    W['x_out'] = nc.dram_tensor("x_out", [T_LOC, D], F32, kind="ExternalOutput").ap()
    W['lnag_in'] = nc.dram_tensor("lnag_in", [D, T_LOC], BF16).ap()
    W['lnag_out'] = nc.dram_tensor("lnag_out", [NCORE * D, T_LOC], BF16).ap()
    W['u_dram'] = nc.dram_tensor("u_dram", [256, SEQ2], F32).ap()
    W['gate_dram'] = nc.dram_tensor("gate_dram", [256, SEQ2], F32).ap()
    W['usil_dram'] = nc.dram_tensor("usil_dram", [256, SEQ2], F32).ap()
    W['pp_dram'] = nc.dram_tensor("pp_dram", [96, SEQ2], F32).ap()
    W['params'] = nc.dram_tensor("params", [96, SEQ2], F32).ap()
    W['zin'] = nc.dram_tensor("zin", [2048, T_LOC], BF16).ap()
    W['zout'] = nc.dram_tensor("zout", [2048, T_LOC], BF16).ap()
    with contextlib.ExitStack() as es:
        ctx = _ctx(nc, es)
        ident16 = emit_identity(ctx, "ident16", BF16, 128)
        H = dict(ident16=ident16, NT=NT, T_LOC=T_LOC)
        done = emit_ssm(ctx, H, W, seq_glob=SEQ)

        @ctx.sec('gpsimd')
        def _(g):
            g.wait('xo0', done[0])
            g.wait('xo1', done[1])
    return nc


def _prog(name, builder):
    if name not in _prog_cache:
        _prog_cache[name] = builder()
    return _prog_cache[name]


def _chunks(c):
    return (slice(c * CH, (c + 1) * CH), slice((7 - c) * CH, (8 - c) * CH))


def _run(nc, in_maps):
    from concourse.bass_utils import run_bass_kernel_spmd
    return run_bass_kernel_spmd(nc, in_maps, core_ids=list(range(NCORE))).results


def _shard_x(xg):
    out = []
    for c in range(NCORE):
        s0, s1 = _chunks(c)
        out.append(np.ascontiguousarray(np.concatenate([xg[0, s0], xg[1, s1]], 0)))
    return out


def _unshard_x(rs, key="x_out"):
    xg = np.zeros((2, SEQ, D), np.float32)
    for c in range(NCORE):
        s0, s1 = _chunks(c)
        xg[0, s0] = rs[c][key][:CH]
        xg[1, s1] = rs[c][key][CH:]
    return xg


def step_retnet(inp, xg):
    nc = _prog("r1", build_r1)
    xs = _shard_x(xg)
    im = [dict(x_in=xs[c], Wq=inp["L0_attn_a_weight_q"], Wk=inp["L0_attn_a_weight_k"],
               Wv=inp["L0_attn_a_weight_v"],
               theta=np.ascontiguousarray(np.asarray(inp["L0_attn_a_theta"])[:, None]))
          for c in range(NCORE)]
    r1 = _run(nc, im)
    kgl = np.zeros((2, D, SEQ), ml_dtypes.bfloat16)
    vgl = np.zeros((2, SEQ, D), ml_dtypes.bfloat16)
    for c in range(NCORE):
        s0, s1 = _chunks(c)
        kgl[0][:, s0] = r1[c]["k_out"][:, :CH]
        kgl[1][:, s1] = r1[c]["k_out"][:, CH:]
        vgl[0][s0] = r1[c]["v_out"][:CH]
        vgl[1][s1] = r1[c]["v_out"][CH:]
    WINR = RET_WIN + CH
    nc = _prog("r2", build_r2)
    im = []
    for c in range(NCORE):
        k_win = np.zeros((2, D, WINR), ml_dtypes.bfloat16)
        v_win = np.zeros((2, WINR, D), ml_dtypes.bfloat16)
        for bi, c2 in ((0, c), (1, 7 - c)):
            end = (c2 + 1) * CH
            lo = end - WINR
            pad = max(0, -lo)
            k_win[bi][:, pad:] = kgl[bi][:, max(0, lo):end]
            v_win[bi][pad:] = vgl[bi][max(0, lo):end]
        im.append(dict(x_in=xs[c], q_fm=r1[c]["q_out"], k_win=k_win, v_win=v_win))
    return _unshard_x(_run(nc, im))


def step_moe(inp, xg):
    nc = _prog("moe", build_moe)
    xs = _shard_x(xg)
    im = []
    for e in range(NCORE):
        onehot = np.zeros((8, 1), np.float32)
        onehot[e, 0] = 1.0
        im.append(dict(x_in=xs[e], gate_W=inp["L0_ffn_gate_W"],
                       W1_e=inp["L0_ffn_W1"][e], b1_e=inp["L0_ffn_b1"][e][None, :],
                       W2_e=inp["L0_ffn_W2"][e], b2_e=inp["L0_ffn_b2"][e][None, :],
                       eselc=onehot))
    return _unshard_x(_run(nc, im))


def step_attnb(inp, xg):
    nc = _prog("b1", build_b1)
    xs = _shard_x(xg)
    im = [dict(x_in=xs[c], Wq=inp["L0_attn_b_Wq"], Wkv=inp["L0_attn_b_Wkv"],
               Wk=inp["L0_attn_b_Wk"], Wv=inp["L0_attn_b_Wv"],
               qg_row=np.ascontiguousarray(np.tile(inp["L0_attn_b_q_g"], 16))[None, :],
               kg_row=np.ascontiguousarray(np.tile(inp["L0_attn_b_k_g"], 16))[None, :])
          for c in range(NCORE)]
    r1b = _run(nc, im)
    kglb = np.zeros((2, 16, 34, SEQ), ml_dtypes.bfloat16)
    vglb = np.zeros((2, SEQ, 512), ml_dtypes.bfloat16)
    for c in range(NCORE):
        s0, s1 = _chunks(c)
        kglb[0][:, :, s0] = r1b[c]["k_pre"][:, :, :CH]
        kglb[1][:, :, s1] = r1b[c]["k_pre"][:, :, CH:]
        vglb[0][s0] = r1b[c]["v_pre"][:CH]
        vglb[1][s1] = r1b[c]["v_pre"][CH:]
    WINB = SEQ + CH
    wo_pk = np.zeros((D, D), np.float32)
    Wo = inp["L0_attn_b_Wo"]
    for g in range(8):
        wo_pk[g * 128:g * 128 + 32] = Wo[2 * g * 64:2 * g * 64 + 32]
        wo_pk[g * 128 + 64:g * 128 + 96] = Wo[(2 * g + 1) * 64:(2 * g + 1) * 64 + 32]
    nc = _prog("b2", build_b2)
    im = []
    for c in range(NCORE):
        k_win = np.zeros((2, 16, 34, WINB), ml_dtypes.bfloat16)
        k_win[:, :, 33, :] = ml_dtypes.bfloat16(ATTN_PAD)
        v_win = np.zeros((2, WINB, 16, 33), ml_dtypes.bfloat16)
        for bi, c2 in ((0, c), (1, 7 - c)):
            end = (c2 + 1) * CH
            lo = end - WINB
            pad = max(0, -lo)
            k_win[bi][:, :, pad:] = kglb[bi][:, :, max(0, lo):end]
            v_win[bi][pad:, :, 0:32] = vglb[bi][max(0, lo):end].reshape(-1, 16, 32)
            v_win[bi][pad:, :, 32] = ml_dtypes.bfloat16(1.0)
        im.append(dict(x_in=xs[c], q_pre=r1b[c]["q_pre"], k_win=k_win,
                       v_win=v_win, wo_pk=wo_pk))
    return _unshard_x(_run(nc, im))


def step_attna(inp, xg):
    nc = _prog("a1", build_a1)
    xs = _shard_x(xg)
    im = [dict(x_in=xs[c], Wqkv=inp["L1_attn_a_Wqkv"],
               bqkv=inp["L1_attn_a_bqkv"][None, :]) for c in range(NCORE)]
    r1a = _run(nc, im)
    kgla = np.zeros((2, 16, 66, SEQ), ml_dtypes.bfloat16)
    vgla = np.zeros((2, SEQ, D), ml_dtypes.bfloat16)
    tsg = np.zeros((2, NTG, D), np.float32)
    for c in range(NCORE):
        s0, s1 = _chunks(c)
        kgla[0][:, :, s0] = r1a[c]["k_pre"][:, :, :CH]
        kgla[1][:, :, s1] = r1a[c]["k_pre"][:, :, CH:]
        vgla[0][s0] = r1a[c]["v_pre"][:CH]
        vgla[1][s1] = r1a[c]["v_pre"][CH:]
        ts = r1a[c]["tilesums"]
        tsg[0][c * NTB:(c + 1) * NTB] = ts[:NTB]
        tsg[1][(7 - c) * NTB:(8 - c) * NTB] = ts[NTB:]
    WINA = SEQ + CH
    tsaug = np.zeros((2, NTG, 1152), np.float32)
    tsaug[:, :, 0:1024] = tsg
    tsaug[:, :, 1024] = 128.0
    nc = _prog("a2", build_a2)
    im = []
    for c in range(NCORE):
        s0, s1 = _chunks(c)
        k_win = np.zeros((2, 16, 66, WINA), ml_dtypes.bfloat16)
        k_win[:, :, 65, :] = ml_dtypes.bfloat16(ATTN_PAD)
        v_win = np.zeros((2, WINA, 16, 65), ml_dtypes.bfloat16)
        for bi, c2 in ((0, c), (1, 7 - c)):
            end = (c2 + 1) * CH
            lo = end - WINA
            pad = max(0, -lo)
            k_win[bi][:, :, pad:] = kgla[bi][:, :, max(0, lo):end]
            v_win[bi][pad:, :, 0:64] = vgla[bi][max(0, lo):end].reshape(-1, 16, 64)
            v_win[bi][pad:, :, 64] = ml_dtypes.bfloat16(1.0)
        v_aug = np.zeros((T_LOC, 1152), ml_dtypes.bfloat16)
        v_aug[:, 0:1024] = np.concatenate([vgla[0][s0], vgla[1][s1]], 0)
        v_aug[:, 1024] = ml_dtypes.bfloat16(1.0)
        msel = np.zeros((2, NTG, NTB), np.float32)
        for bi, c2 in ((0, c), (1, 7 - c)):
            for j in range(NTB):
                gt = c2 * NTB + j
                msel[bi, gt + 1:, j] = 1.0
        im.append(dict(x_in=xs[c], q_pre=r1a[c]["q_pre"], k_win=k_win, v_win=v_win,
                       v_aug=v_aug, tsaug=tsaug, msel=msel,
                       wr=inp["L1_attn_a_Wr"], br=inp["L1_attn_a_br"][None, :]))
    return _unshard_x(_run(nc, im))


def step_lf(inp, xg):
    nc = _prog("lf", build_lf)
    xs = _shard_x(xg)
    im = []
    for c in range(NCORE):
        widx = np.zeros((T_LOC, 1), np.uint32)
        for bi, c2 in ((0, c), (1, 7 - c)):
            pos = np.arange(CH) + c2 * CH
            widx[bi * CH:(bi + 1) * CH, 0] = bi * SEQ + pos
        im.append(dict(
            x_in=xs[c], rt_row=inp["L1_ffn_routing_token"][None, :],
            gl=inp["L1_ffn_gamma_light"][:, None], gh=inp["L1_ffn_gamma_heavy"][:, None],
            w1l=inp["L1_ffn_w1_light"], b1l=inp["L1_ffn_b1_light"][None, :],
            w2l=inp["L1_ffn_w2_light"], b2l=inp["L1_ffn_b2_light"][None, :],
            w1h=inp["L1_ffn_w1_heavy"], b1h=inp["L1_ffn_b1_heavy"][None, :],
            w2h=inp["L1_ffn_w2_heavy"], b2h=inp["L1_ffn_b2_heavy"][None, :],
            widx=widx))
    return _unshard_x(_run(nc, im))


def step_ssm(inp, xg):
    nc = _prog("sm", build_sm)
    xs = _shard_x(xg)
    W_in = inp["L1_attn_b_W_in"]
    im = []
    for c in range(NCORE):
        chs = slice(c * 256, (c + 1) * 256)
        im.append(dict(
            x_in=xs[c],
            w_in_uT=np.ascontiguousarray(W_in[0:2048][chs].T),
            w_in_gT=np.ascontiguousarray(W_in[2048:4096][chs].T),
            conv_c=np.ascontiguousarray(inp["L1_attn_b_conv_w"][chs]),
            wssmT=np.ascontiguousarray(inp["L1_attn_b_W_ssm"][:, chs].T),
            wdtT=np.ascontiguousarray(inp["L1_attn_b_W_dt"][chs].T),
            b_dt_c=np.ascontiguousarray(inp["L1_attn_b_b_dt"][chs][:, None]),
            logA_c=np.ascontiguousarray(inp["L1_attn_b_log_A"][chs]),
            D_c=np.ascontiguousarray(inp["L1_attn_b_D"][chs][:, None]),
            woutT=np.ascontiguousarray(inp["L1_attn_b_W_out"].T)))
    return _unshard_x(_run(nc, im))


STEPS = [step_retnet, step_moe, step_attnb, step_moe, step_attna, step_lf,
         step_ssm, step_lf]


def kernel(**inp):
    xg = np.asarray(inp["L0_attn_a_x"], np.float32)
    for fn in STEPS:
        xg = fn(inp, xg)
    return xg.astype(np.float32)
